# version 07
# baseline (speedup 1.0000x reference)
import math
from contextlib import ExitStack

import numpy as np
import ml_dtypes
import concourse.bass as bass
import concourse.mybir as mybir
from concourse.bass_utils import run_bass_kernel_spmd

F32 = mybir.dt.float32
BF16 = mybir.dt.bfloat16
AF = mybir.ActivationFunctionType
ALU = mybir.AluOpType
AX = mybir.AxisListType

B, S, E, H, D = 4, 4096, 1024, 16, 64
DH, QC, NKT = 4096, 2048, 32
THRESH = 64.0
EPS = 1e-5

_xb = (2.0 ** 8) ** (1.0 / H)
SLOPES64 = [1.0 / _xb ** (i + 1) for i in range(H)]
SLOPES32 = [float(np.float32(s)) for s in SLOPES64]

def _kt0(h):
    s = SLOPES32[h]
    lo = 3968.0 - THRESH / s
    return 0 if lo <= 0 else min(NKT - 1, int(math.ceil(lo / 128.0)))

KT0 = [_kt0(h) for h in range(H)]
PAIRS = [(2 * i, 2 * i + 1) for i in range(H // 2)]
PAIR_KT0 = [min(KT0[a], KT0[b]) for a, b in PAIRS]
UW = {h: 6016 - PAIR_KT0[h // 2] * 128 for h in range(0, H, 2)}
M_H = [SLOPES32[h] * 4095.0 for h in range(H)]

NT1 = 48
LD0 = 16
NQK = 48
NVG = 32

GQ = []
for _hh in (0, 1):
    for _pi in range(4):
        _P = 4 * _hh + _pi
        for _qc in range(4):
            GQ.append(dict(hh=_hh, pi=_pi, P=_P, hA=PAIRS[_P][0], hB=PAIRS[_P][1],
                           qc=_qc, kts=list(range(PAIR_KT0[_P], NKT))))
EV_START = {}
_e = 0
for _gi, _g in enumerate(GQ):
    EV_START[_gi] = _e
    _e += len(_g["kts"])
NEV = _e
NGRP = len(GQ)
EV_HALF0_END = EV_START[16]
PAIR_EV_START = {P: EV_START[4 * P] for P in range(8)}


def build_nc(dbg=None):
    nc = bass.Bass("TRN2", target_bir_lowering=False, debug=False)
    es = ExitStack()

    x_seq = nc.dram_tensor("x_seq", [S, E], F32, kind="ExternalInput").ap()
    x_own = nc.dram_tensor("x_own", [QC, E], F32, kind="ExternalInput").ap()
    WqD = nc.dram_tensor("Wq", [E, E], F32, kind="ExternalInput").ap()
    WkD = nc.dram_tensor("Wk", [E, E], F32, kind="ExternalInput").ap()
    WvD = nc.dram_tensor("Wv", [E, E], F32, kind="ExternalInput").ap()
    WoD = nc.dram_tensor("Wo", [E, E], F32, kind="ExternalInput").ap()
    cvec = nc.dram_tensor("cvec", [128, 144], F32, kind="ExternalInput").ap()
    cvec2 = nc.dram_tensor("cvec2", [1, 3072], F32, kind="ExternalInput").ap()
    UtD = {h: nc.dram_tensor(f"U{h}", [128, UW[h]], F32, kind="ExternalInput").ap()
           for h in range(0, H, 2)}
    w1sl = nc.dram_tensor("w1sl", [8, 8, 128, 512], F32, kind="ExternalInput").ap()
    w2sl = nc.dram_tensor("w2sl", [8, 8, 128, 512], F32, kind="ExternalInput").ap()
    h1sl = nc.dram_tensor("h1sl", [32, 128, 32, 128], F32, kind="ExternalInput").ap()
    h2sl = nc.dram_tensor("h2sl", [32, 128, 32, 128], F32, kind="ExternalInput").ap()
    w3sl = nc.dram_tensor("w3sl", [32, 128, 1024], F32, kind="ExternalInput").ap()
    out = nc.dram_tensor("out", [QC, E], F32, kind="ExternalOutput").ap()
    xnf_d = nc.dram_tensor("xnf_d", [8, 128, S], BF16, kind="Internal").ap()
    xno_d = nc.dram_tensor("xno_d", [8, 128, QC], BF16, kind="Internal").ap()
    acc_d = nc.dram_tensor("acc_d", [16, 128, E], F32, kind="Internal").ap()
    dbg_outs = {}
    if dbg == "p1":
        dbg_outs["xnf_o"] = nc.dram_tensor("xnf_o", [8, 128, S], BF16, kind="ExternalOutput").ap()
        dbg_outs["xno_o"] = nc.dram_tensor("xno_o", [8, 128, QC], BF16, kind="ExternalOutput").ap()
    if dbg in ("p2", "p2p"):
        dbg_outs["kf_o"] = nc.dram_tensor("kf_o", [128, 16384], BF16, kind="ExternalOutput").ap()
        dbg_outs["vt_o"] = nc.dram_tensor("vt_o", [128, 16640], BF16, kind="ExternalOutput").ap()
        dbg_outs["qf_o"] = nc.dram_tensor("qf_o", [128, 8192], BF16, kind="ExternalOutput").ap()
    if dbg in ("p2", "p2c"):
        dbg_outs["ctx_o"] = nc.dram_tensor("ctx_o", [128, 16384], BF16, kind="ExternalOutput").ap()
    if dbg == "p3":
        dbg_outs["acc_o"] = nc.dram_tensor("acc_o", [16, 128, E], F32, kind="ExternalOutput").ap()
    if dbg in ("p4a", "p4b", "p4c"):
        dbg_outs["hb_o"] = nc.dram_tensor("hb_o", [128, 32768], BF16, kind="ExternalOutput").ap()
    if dbg == "p4d":
        dbg_outs["acc_o"] = nc.dram_tensor("acc_o", [16, 128, E], F32, kind="ExternalOutput").ap()

    sbA = lambda *a: nc.alloc_sbuf_tensor(*a).ap()
    sem = lambda n: es.enter_context(nc.semaphore(n))

    cv = sbA("cv", [128, 144], F32)
    idn = sbA("idn", [128, 128], F32)
    p2b = sbA("p2b", [128, 256], F32)
    p2a = sbA("p2a", [128, 256], F32)
    bvb = sbA("bvb", [128, 1024], BF16)
    bob = sbA("bob", [128, 1024], BF16)
    s3b = sbA("s3b", [128, 1024], BF16)
    stt = sbA("stt", [128, 16], F32)
    idnb = sbA("idnb", [128, 128], BF16)
    ones1 = sbA("ones1", [1, 128], F32)
    io32 = p2a[:, 0:32]
    kc = stt[:, 7:8]

    AR16 = nc.alloc_sbuf_tensor("AR16", [128, 84224], BF16).ap()
    AR32 = nc.alloc_sbuf_tensor("AR32", [128, 8192], F32).ap()

    def at(name, shape, dt, off):
        ds = 2 if dt == BF16 else 4
        base = AR16 if dt == BF16 else AR32
        nel = 1
        for s in shape[1:]:
            nel *= s
        assert off % ds == 0, (name, off)
        v = base[:, off // ds: off // ds + nel]
        if len(shape) == 3:
            v = v.rearrange("p (a b) -> p a b", a=shape[1])
        elif len(shape) == 4:
            v = v.rearrange("p (a b c) -> p a b c", a=shape[1], b=shape[2])
        elif len(shape) == 5:
            v = v.rearrange("p (a b c d) -> p a b c d", a=shape[1], b=shape[2],
                            c=shape[3])
        return v
    xin = at("xin", [128, 2048], F32, 0)
    xnt = at("xnt", [128, 2048], F32, 8192)
    xs = at("xs", [128, 2, 8, 512], BF16, 0)
    sqs = at("sqs", [128, 1024], F32, 16384)
    kf = at("kf", [128, 16384], BF16, 0)
    vt = at("vt", [128, 32, 8, 65], BF16, 32768)
    qf = at("qf", [128, 8192], BF16, 66048)
    ub = at("ub", [128, 4480], F32, 8192)
    tsc = at("tsc", [128, 2, 512], F32, 26112)
    c2 = at("c2", [1, 3072], F32, 20480)
    eT = at("eT", [128, 2048], BF16, 98816)
    
    rrow = at("rrow", [1, 1024], F32, 0)
    rbc = at("rbc", [128, 1024], F32, 4096)
    ctxf = at("ctxf", [128, 16384], BF16, 102912)
    xnfb = at("xnfb", [128, 2, 8, 1024], BF16, 135680)
    xnob = at("xnob", [128, 8, 2048], BF16, 135680)
    wqb = at("wqb", [128, 4096], BF16, 16384)
    wkb = at("wkb", [128, 4096], BF16, 82432)
    wvb = at("wvb", [128, 4096], BF16, 90624)
    wob = at("wob", [128, 8192], BF16, 135680)
    xacc = at("xacc", [128, 2048], F32, 0)
    aout = at("aout", [128, 2048], F32, 8192)
    hbuf = at("hbuf", [128, 32, 1024], BF16, 0)
    h2b = at("h2b", [128, 32, 1024], BF16, 65536)
    wfA = at("wfA", [128, 2, 2, 8, 512], BF16, 65536)
    wf3 = at("wf3", [128, 32, 1024], BF16, 65536)
    xnT = at("xnT", [128, 8, 1024], BF16, 147456)
    wfB = at("wfB", [128, 2, 32, 128], BF16, 131072)
    silu = at("silu", [128, 2, 1024], BF16, 163840)
    acci = at("acci", [128, 2, 1024], F32, 16384)
    uout = at("uout", [128, 2, 1024], F32, 24576)
    wstg = at("wstg", [128, 2, 2, 512], F32, 20480)

    PSA = nc.alloc_psum_tensor("PSA", [128, 2048], F32).ap()
    PSB = nc.alloc_psum_tensor("PSB", [128, 2048], F32).ap()
    pT = [PSB[:, 0:1024], PSB[:, 1024:2048]]
    pB, pB2 = PSA[:, 0:512], PSA[:, 512:1024]
    pSC = [PSA[:, j * 512:(j + 1) * 512] for j in range(4)]
    pCA = PSB[:, 0:1024]
    pCB = PSB[:, 1024:2048]

    s_ld = sem("s_ld"); s_lc = sem("s_lc"); s_lb = sem("s_lb"); s_ub = sem("s_ub")
    s_st = sem("s_st"); s_ge = sem("s_ge"); s_wl = sem("s_wl")
    s_pe = sem("s_pe"); s_se = sem("s_se"); s_s2 = sem("s_s2")
    s_ve = sem("s_ve"); s_vv = sem("s_vv"); s_v2 = sem("s_v2")
    s_pp = sem("s_pp"); s_pq = sem("s_pq"); s_pk = sem("s_pk"); s_pv = sem("s_pv")
    s_qk = sem("s_qk"); s_vst = sem("s_vst"); s_exa = sem("s_exa"); s_exb = sem("s_exb")
    s_av = sem("s_av"); s_rv = sem("s_rv"); s_bc = sem("s_bc"); s_cp = sem("s_cp")
    s_ml = sem("s_ml")
    s_3m = sem("s_3m"); s_3v = sem("s_3v"); s_3x = sem("s_3x"); s_3a = sem("s_3a")
    s_4w = sem("s_4w"); s_4x = sem("s_4x"); s_4m = sem("s_4m"); s_4v = sem("s_4v")
    s_4s = sem("s_4s"); s_4o = sem("s_4o"); s_4a = sem("s_4a")
    s_cv = sem("s_cv")
    s_x2 = [sem("s_x0"), sem("s_x1")]
    s_sb = [sem("s_sb0"), sem("s_sb1")]
    s_xf = [sem("s_xf0"), sem("s_xf1")]
    s_3x2 = [sem("s_3x0"), sem("s_3x1")]
    s_3a2 = [sem("s_3a0"), sem("s_3a1")]
    s_wa = [sem("s_wa0"), sem("s_wa1")]
    s_wc = sem("s_wc"); s_ws = [sem("s_ws0"), sem("s_ws1")]
    s_wb = [sem("s_wb0"), sem("s_wb1")]
    s_w3 = sem("s_w3")
    s_xT = sem("s_xT")
    s_ac = [sem("s_ac0"), sem("s_ac1")]
    s_4o2 = [sem("s_4o0"), sem("s_4o1")]

    def vslot_of(n):
        return [pCA, pCB][n % 2][:, (n % 4 // 2) * 512:(n % 4 // 2) * 512 + 512]

    def d_done(g):
        tt, qq = divmod(g, 8)
        return tt * 40 + 32 + qq + 1

    with nc.Block() as blk:

        @blk.sync
        def _(sy):
            sy.dma_start(cv[:, :], cvec).then_inc(s_cv, 16)
            sy.dma_start(c2[0:1, :], cvec2).then_inc(s_lc, 16)
            for i in range(NT1):
                ip = i % 2
                if i >= 2:
                    sy.wait_ge(s_ve, i - 1)
                    sy.wait_ge(s_se, i - 1)
                src = x_own[i * 128:(i + 1) * 128, :] if i < 16 else \
                      x_seq[(i - 16) * 128:(i - 15) * 128, :]
                sy.dma_start(xin[:, ip * 1024:(ip + 1) * 1024], src).then_inc(s_x2[ip], 16)
                if i >= 5 and (i - 5) % 4 == 0:
                    g = (i - 5) // 4
                    sy.wait_ge(s_v2, 5 + 4 * g + 4)
                    dst = xno_d[:, :, 512 * g:512 * (g + 1)] if g < 4 else \
                          xnf_d[:, :, 512 * (g - 4):512 * (g - 3)]
                    sy.dma_start(dst.rearrange("d p j -> p d j"),
                                 xs[:, g % 2, :, :]).then_inc(s_sb[g % 2], 16)
                    if g == 3 and dbg != "p1":
                        sy.wait_ge(s_sb[0], 32)
                        sy.wait_ge(s_sb[1], 32)
                        sy.dma_start(xnob[:, :, :],
                                     xno_d.rearrange("d p j -> p d j")).then_inc(s_lb, 16)
            for g in (11,):
                sy.wait_ge(s_v2, 5 + 4 * g + 4)
                sy.dma_start(xnf_d[:, :, 512 * (g - 4):512 * (g - 3)]
                             .rearrange("d p j -> p d j"),
                             xs[:, g % 2, :, :]).then_inc(s_sb[g % 2], 16)
            if dbg == "p1":
                sy.wait_ge(s_sb[0], 16 * 6)
                sy.wait_ge(s_sb[1], 16 * 6)
                sy.dma_start(dbg_outs["xnf_o"], xnf_d).then_inc(s_st, 16)
                sy.dma_start(dbg_outs["xno_o"], xno_d).then_inc(s_st, 16)
                sy.wait_ge(s_st, 16 * 2)
                return
            for hh in ((0,) if dbg == "p2p" else (0, 1)):
                if hh == 1:
                    sy.wait_ge(s_pp, NQK)
                    sy.wait_ge(s_pq, NVG)
                    sy.dma_start(xnob[:, :, :],
                                 xno_d.rearrange("d p j -> p d j")).then_inc(s_lb, 16)
                for b in range(4):
                    gidx = hh * 4 + b
                    sy.wait_ge(s_sb[0], 16 * (b + 3))
                    sy.wait_ge(s_sb[1], 16 * (b + 3))
                    sy.wait_ge(s_pp, hh * NQK + 16)
                    if gidx >= 2:
                        hh2, b2 = divmod(gidx - 2, 4)
                        sy.wait_ge(s_pp, hh2 * NQK + 16 + (b2 + 1) * 8)
                        sy.wait_ge(s_pq, hh2 * NVG + (b2 + 1) * 8)
                    sy.dma_start(xnfb[:, b % 2, :, :],
                                 xnf_d[:, :, b * 1024:(b + 1) * 1024]
                                 .rearrange("d p j -> p d j")).then_inc(s_xf[b % 2], 16)
                if dbg == "p2p":
                    continue
                for gi in range(hh * 16, (hh + 1) * 16):
                    g = GQ[gi]
                    hA, qc = g["hA"], g["qc"]
                    Wp = (31 - PAIR_KT0[g["P"]]) * 128 + 512
                    if gi > 0:
                        sy.wait_ge(s_qk, EV_START[gi])
                    if gi == hh * 16:
                        sy.wait_ge(s_pp, (hh + 1) * NQK)
                        sy.wait_ge(s_pq, (hh + 1) * NVG)
                    sy.dma_start(ub[:, 0:Wp],
                                 UtD[hA][:, qc * 512: qc * 512 + Wp]).then_inc(s_ub, 16)
            if dbg == "p2p":
                sy.wait_ge(s_pk, NQK)
                sy.wait_ge(s_pv, NVG)
                sy.dma_start(dbg_outs["kf_o"], kf[:, :]).then_inc(s_st, 16)
                sy.dma_start(dbg_outs["vt_o"],
                             vt[:, :, :, :].rearrange("p a b c -> p (a b c)")).then_inc(s_st, 16)
                sy.dma_start(dbg_outs["qf_o"], qf[:, :]).then_inc(s_st, 16)
                sy.wait_ge(s_st, 16 * 3)
                return
            if dbg == "p2":
                sy.wait_ge(s_pk, 2 * NQK)
                sy.wait_ge(s_pv, 2 * NVG)
                sy.dma_start(dbg_outs["kf_o"], kf[:, :]).then_inc(s_st, 16)
                sy.dma_start(dbg_outs["vt_o"],
                             vt[:, :, :, :].rearrange("p a b c -> p (a b c)")).then_inc(s_st, 16)
                sy.dma_start(dbg_outs["qf_o"], qf[:, :]).then_inc(s_st, 16)
            if dbg in ("p2", "p2c"):
                sy.wait_ge(s_ml, 64)
                sy.dma_start(dbg_outs["ctx_o"], ctxf[:, :]).then_inc(s_st, 16)
                extra = 4 if dbg == "p2" else 1
                sy.wait_ge(s_st, 16 * extra)
                return
            for qt in range(16):
                if qt == 0:
                    sy.wait_ge(s_qk, NEV)
                    sy.wait_ge(s_ml, 2 * NGRP)
                if qt >= 2:
                    sy.wait_ge(s_3v, qt - 1)
                sy.dma_start(xacc[:, (qt % 2) * 1024:(qt % 2 + 1) * 1024],
                             x_own[qt * 128:(qt + 1) * 128, :]).then_inc(s_3x2[qt % 2], 16)
                sy.wait_ge(s_3v, qt + 1)
                sy.dma_start(acc_d[qt], aout[:, (qt % 2) * 1024:(qt % 2 + 1) * 1024]
                             ).then_inc(s_3a2[qt % 2], 16)
            if dbg == "p3":
                sy.wait_ge(s_3a2[0], 16 * 8)
                sy.wait_ge(s_3a2[1], 16 * 8)
                sy.dma_start(dbg_outs["acc_o"], acc_d).then_inc(s_st, 16)
                sy.wait_ge(s_st, 16)
                return
            if dbg in ("p4a", "p4b", "p4c"):
                sy.wait_ge(s_3m, 16)
                sy.wait_ge(s_3a2[0], 16 * 8)
                sy.wait_ge(s_3a2[1], 16 * 8)
                sy.dma_start(xnT[:, :, :],
                             xno_d[:, :, 0:1024].rearrange("d p j -> p d j")).then_inc(s_xT, 16)
                if dbg == "p4a":
                    sy.wait_ge(s_4v, 32)
                    sy.dma_start(dbg_outs["hb_o"], hbuf[:, :, :].rearrange("p a b -> p (a b)")).then_inc(s_st, 16)
                elif dbg == "p4b":
                    sy.wait_ge(s_4s, 64)
                    sy.dma_start(dbg_outs["hb_o"], h2b[:, :, :].rearrange("p a b -> p (a b)")).then_inc(s_st, 16)
                else:
                    sy.wait_ge(s_4s, 96)
                    sy.dma_start(dbg_outs["hb_o"], hbuf[:, :, :].rearrange("p a b -> p (a b)")).then_inc(s_st, 16)
                sy.wait_ge(s_st, 16)
                return
            n4x = 0
            n4o = 0
            for T in ((0,) if dbg == "p4f" else (0, 1)):
                if T == 0:
                    sy.wait_ge(s_3m, 16)
                    sy.wait_ge(s_3a2[0], 16 * 8)
                    sy.wait_ge(s_3a2[1], 16 * 8)
                else:
                    sy.wait_ge(s_4m, 64)
                sy.dma_start(xnT[:, :, :],
                             xno_d[:, :, T * 1024:(T + 1) * 1024]
                             .rearrange("d p j -> p d j")).then_inc(s_xT, 16)
                n4x += 1
                for qt in range(8):
                    gq = T * 8 + qt
                    if gq >= 2:
                        sy.wait_ge(s_4v, d_done(gq - 2))
                    sy.dma_start(acci[:, gq % 2, :], acc_d[gq]).then_inc(s_ac[gq % 2], 16)
                    n4x += 1
                    sy.wait_ge(s_4v, d_done(gq))
                    sy.dma_start(out[(T * 8 + qt) * 128:(T * 8 + qt + 1) * 128, :],
                                 uout[:, gq % 2, :]).then_inc(s_4o2[gq % 2], 16)
            sy.wait_ge(s_4o2[0], 16 * (4 if dbg == "p4f" else 8))
            sy.wait_ge(s_4o2[1], 16 * (4 if dbg == "p4f" else 8))
            if dbg == "p4d":
                sy.dma_start(dbg_outs["acc_o"], acc_d).then_inc(s_st, 16)
                sy.wait_ge(s_st, 16)

        @blk.gpsimd
        def _(gp):
            gp.iota(io32[:, :], [[128, 32]], base=0, channel_multiplier=1,
                    allow_small_or_imprecise_dtypes=True).then_inc(s_ge, 1)
            gp.wait_ge(s_v2, 1)
            gp.affine_select(idn[:, :], idn[:, :], [[-1, 128]], ALU.is_equal,
                             0.0, base=0, channel_multiplier=1)
            gp.drain()
            gp.tensor_copy(idnb[:, :], idn[:, :]).then_inc(s_ge, 2)
            if dbg == "p1":
                return
            n_ws = [0, 0]
            n_wc = [0]

            def wload(WD, dstb, colr, prewait=None):
                ncol = colr[1] - colr[0]
                dpc = 512 // ncol * 2 if ncol <= 512 else 1
                nch = 8 // dpc
                for ch in range(nch):
                    sl = ch % 2
                    if prewait is not None and ch == 0:
                        prewait()
                    if n_wc[0] >= 2:
                        gp.wait_ge(s_wc, n_wc[0] - 1)
                    srcw = WD[128 * dpc * ch:128 * dpc * (ch + 1), colr[0]:colr[1]] \
                        .rearrange("(d p) j -> p d j", p=128)
                    wsf = wstg[:, sl, :, :].rearrange("p a b -> p (a b)") \
                        .rearrange("p (d j) -> p d j", j=ncol)[:, 0:dpc, :]
                    gp.dma_start(wsf, srcw).then_inc(s_ws[sl], 16)
                    n_ws[sl] += 1
                    gp.wait_ge(s_ws[sl], 16 * n_ws[sl])
                    gp.tensor_copy(dstb[:, :].rearrange("p (d j) -> p d j", d=8)
                                   [:, dpc * ch:dpc * (ch + 1), :], wsf).then_inc(s_wc, 1)
                    n_wc[0] += 1

            for hh in ((0,) if dbg == "p2p" else (0, 1)):
                if hh == 1:
                    gp.wait_ge(s_qk, EV_HALF0_END)
                    gp.wait_ge(s_exa, EV_HALF0_END)
                else:
                    gp.wait_ge(s_pe, 3)
                for WD, dstb in ((WqD, wqb), (WkD, wkb), (WvD, wvb)):
                    wload(WD, dstb, (hh * 512, (hh + 1) * 512))
            if dbg in ("p2", "p2c", "p2p"):
                return
            def _wo_pre():
                gp.wait_ge(s_pp, 2 * NQK)
                gp.wait_ge(s_pq, 2 * NVG)
                gp.wait_ge(s_qk, NEV)
                gp.wait_ge(s_exa, NEV)
            wload(WoD, wob, (0, 1024), prewait=_wo_pre)
            if dbg == "p3":
                return
            n4w = 0
            for T in ((0,) if dbg in ("p4a", "p4b", "p4c", "p4f") else (0, 1)):
                for hb in range(8):
                    if T == 0 and hb == 0:
                        gp.wait_ge(s_av, NEV)
                        gp.wait_ge(s_ml, 64)
                        gp.wait_ge(s_3m, 16)
                    if hb >= 2:
                        gp.wait_ge(s_4m, T * 208 + (hb - 1) * 8)
                    elif T > 0:
                        gp.wait_ge(s_4m, T * 208)
                    for w, sl in ((0, w1sl), (1, w2sl)):
                        gp.dma_start(wfA[:, w, hb % 2, :, :],
                                     sl[hb].rearrange("d p j -> p d j")).then_inc(s_wa[hb % 2], 16)
                        n4w += 1
                if dbg == "p4a":
                    return
                for ho in range(32):
                    if T == 0 and ho == 0:
                        pass
                    if ho >= 2:
                        gp.wait_ge(s_4m, T * 208 + 64 + (ho - 1) * 2)
                    elif T > 0:
                        gp.wait_ge(s_4m, (T - 1) * 208 + 192)
                    else:
                        gp.wait_ge(s_3m, 16)
                    gp.dma_start(wfB[:, ho % 2, :, :], h1sl[ho]).then_inc(s_wb[ho % 2], 16)
                    n4w += 1
                if dbg == "p4b":
                    return
                for ho in range(32):
                    if ho >= 2:
                        gp.wait_ge(s_4m, T * 208 + 128 + (ho - 1) * 2)
                    else:
                        gp.wait_ge(s_4m, T * 208 + 128)
                    gp.dma_start(wfB[:, ho % 2, :, :], h2sl[ho]).then_inc(s_wb[ho % 2], 16)
                    n4w += 1
                if dbg == "p4c":
                    return
                gp.wait_ge(s_4m, T * 208 + 192)
                for wq in range(4):
                    gp.dma_start(wf3[:, wq * 8:(wq + 1) * 8, :],
                                 w3sl[wq * 8:(wq + 1) * 8].rearrange("h p j -> p h j")
                                 ).then_inc(s_w3, 16)
                    n4w += 1

        @blk.tensor
        def _(pe):
            pe.wait_ge(s_v2, 2)
            pe.wait_ge(s_lc, 16)
            for r in range(3):
                if r > 0:
                    pe.wait_ge(s_v2, 2 + r)
                pe.matmul(pB[:, :], ones1[0:1, :],
                          c2[0:1, r * 1024: r * 1024 + 512], start=True, stop=True)
                pe.matmul(pB2[:, :], ones1[0:1, :],
                          c2[0:1, r * 1024 + 512: (r + 1) * 1024],
                          start=True, stop=True).then_inc(s_pe, 1)
            pe.wait_ge(s_ge, 3)
            n_pp_box = [0]

            def q_proj(hh, lo, hi):
                n_pp = n_pp_box[0]
                if lo == 0:
                    pe.wait_ge(s_wc, hh * 12 + 4)
                    pe.wait_ge(s_lb, (hh + 1) * 16)
                    if hh == 0:
                        pe.wait_ge(s_v2, 5)
                    else:
                        pe.wait_ge(s_ml, 32)
                        pe.wait_ge(s_cp, 32)
                        pe.wait_ge(s_exa, EV_HALF0_END)
                        pe.wait_ge(s_exb, EV_HALF0_END)
                for g in range(lo, hi):
                    dc, tc = g // 4, g % 4
                    slot = pSC[n_pp % 4]
                    if n_pp >= 4:
                        pe.wait_ge(s_pk, n_pp - 3)
                    for dinc in range(8):
                        ins = pe.matmul(slot,
                                        wqb[:, dinc * 512 + dc * 128: dinc * 512 + (dc + 1) * 128],
                                        xnob[:, dinc, tc * 512:(tc + 1) * 512],
                                        start=(dinc == 0), stop=(dinc == 7))
                    ins.then_inc(s_pp, 1)
                    n_pp += 1
                n_pp_box[0] = n_pp

            for i in range(NT1):
                ip = i % 2
                pe.wait_ge(s_ve, i + 1)
                if i >= 2:
                    pe.wait_ge(s_v2, 5 + i - 1)
                for dc in range(8):
                    ins = pe.transpose(pT[ip][:, dc * 128:(dc + 1) * 128],
                                       xnt[:, ip * 1024 + dc * 128: ip * 1024 + (dc + 1) * 128],
                                       idn[:, :])
                ins.then_inc(s_pe, 1)
                if dbg != "p1":
                    if i == 15:
                        q_proj(0, 0, 4)
                    elif 16 <= i < 28:
                        q_proj(0, 4 + (i - 16), 5 + (i - 16))
            if dbg == "p1":
                return
            n_pq = 0
            n_ev = 0
            n_gq = 0
            for hh in ((0,) if dbg == "p2p" else (0, 1)):
                if hh == 1:
                    q_proj(1, 0, 16)
                n_pp = n_pp_box[0]
                pe.wait_ge(s_wc, hh * 12 + 12)
                for b in range(4):
                    pe.wait_ge(s_xf[b % 2], 16 * (hh * 2 + b // 2 + 1))
                    for dc in range(4):
                        for j2 in range(2):
                            slot = pSC[n_pp % 4]
                            if n_pp >= 4:
                                pe.wait_ge(s_pk, n_pp - 3)
                            for dinc in range(8):
                                ins = pe.matmul(slot,
                                                wkb[:, dinc * 512 + dc * 128: dinc * 512 + (dc + 1) * 128],
                                                xnfb[:, b % 2, dinc, j2 * 512:(j2 + 1) * 512],
                                                start=(dinc == 0), stop=(dinc == 7))
                            ins.then_inc(s_pp, 1)
                            n_pp += 1
                    if hh == 0 and b == 0:
                        pe.wait_ge(s_v2, 5 + NT1)
                    for tt in range(8):
                        vslot = vslot_of(n_pq)
                        if n_pq >= 4:
                            pe.wait_ge(s_pv, n_pq - 3)
                        for dinc in range(8):
                            ins = pe.matmul(vslot,
                                            xnfb[:, b % 2, dinc, tt * 128:(tt + 1) * 128],
                                            wvb[:, dinc * 512:(dinc + 1) * 512],
                                            start=(dinc == 0), stop=(dinc == 7))
                        ins.then_inc(s_pq, 1)
                        n_pq += 1
                n_pp_box[0] = n_pp
                if dbg == "p2p":
                    continue
                pe.wait_ge(s_pk, (hh + 1) * NQK)
                pe.wait_ge(s_pv, (hh + 1) * NVG)

                def scoreA(e):
                    return PSA[:, (e % 2) * 1024:(e % 2) * 1024 + 512]
                def scoreB(e):
                    return PSA[:, (e % 2) * 1024 + 512:(e % 2 + 1) * 1024]
                def ctxA(J):
                    return PSB[:, (J % 2) * 1024:(J % 2) * 1024 + 512]
                def ctxB(J):
                    return PSB[:, (J % 2) * 1024 + 512:(J % 2 + 1) * 1024]

                pend = None
                for gi in range(hh * 16, (hh + 1) * 16):
                    g = GQ[gi]
                    pi, qc, P = g["pi"], g["qc"], g["P"]
                    nk = len(g["kts"])
                    q0 = pi * 2048 + qc * 512
                    for ki, kt in enumerate(g["kts"]):
                        e = n_ev
                        first, last = ki == 0, ki == nk - 1
                        if e >= 2:
                            pe.wait_ge(s_exa, e - 1)
                            pe.wait_ge(s_exb, e - 1)
                        if ki == 0 and gi >= 1:
                            pe.wait_ge(s_cp, 2 * (gi - 1))
                        elif ki == 1:
                            pe.wait_ge(s_cp, 2 * gi)
                        if ki == 0:
                            pe.wait_ge(s_ub, 16 * (gi + 1))
                        kA = kf[0:64, pi * 4096 + kt * 128: pi * 4096 + (kt + 1) * 128]
                        kB = kf[64:128, pi * 4096 + kt * 128: pi * 4096 + (kt + 1) * 128]
                        off = 3968 - kt * 128
                        pe.matmul(scoreA(e), kA, qf[0:64, q0:q0 + 512],
                                  start=True, stop=False, tile_position=(0, 0))
                        pe.matmul(scoreA(e), idn[:, :], ub[:, off:off + 512],
                                  start=False, stop=True)
                        pe.matmul(scoreB(e), kB, qf[64:128, q0:q0 + 512],
                                  start=True, stop=True,
                                  tile_position=(64, 0)).then_inc(s_qk, 1)
                        n_ev += 1
                        if pend is not None:
                            pe2, gi2, f2, l2, kt2 = pend
                            g2 = GQ[gi2]
                            pe.wait_ge(s_exa, pe2 + 1)
                            pe.wait_ge(s_exb, pe2 + 1)
                            if f2 and gi2 >= 2:
                                pe.wait_ge(s_ml, 2 * (gi2 - 1))
                            pe.matmul(ctxA(gi2)[0:65, :], vt[:, kt2, 2 * g2["pi"], :],
                                      eT[:, (pe2 % 2) * 512:(pe2 % 2) * 512 + 512],
                                      start=f2, stop=l2)
                            ins = pe.matmul(ctxB(gi2)[0:65, :], vt[:, kt2, 2 * g2["pi"] + 1, :],
                                            eT[:, 1024 + (pe2 % 2) * 512: 1024 + (pe2 % 2) * 512 + 512],
                                            start=f2, stop=l2)
                            ins.then_inc(s_av, 1)
                            if l2:
                                for side in range(2):
                                    pe.wait_ge(s_rv, 2 * gi2 + side + 1)
                                    tgt = scoreA(pe2) if side == 0 else scoreB(pe2)
                                    pe.matmul(tgt, ones1[0:1, :], rrow[0:1, 0:512],
                                              start=True, stop=True).then_inc(s_bc, 1)
                        pend = (e, gi, first, last, kt)
                if pend is not None:
                    pe2, gi2, f2, l2, kt2 = pend
                    g2 = GQ[gi2]
                    pe.wait_ge(s_exa, pe2 + 1)
                    pe.wait_ge(s_exb, pe2 + 1)
                    if f2 and gi2 >= 2:
                        pe.wait_ge(s_ml, 2 * (gi2 - 1))
                    pe.matmul(ctxA(gi2)[0:65, :], vt[:, kt2, 2 * g2["pi"], :],
                              eT[:, (pe2 % 2) * 512:(pe2 % 2) * 512 + 512],
                              start=f2, stop=l2)
                    pe.matmul(ctxB(gi2)[0:65, :], vt[:, kt2, 2 * g2["pi"] + 1, :],
                              eT[:, 1024 + (pe2 % 2) * 512: 1024 + (pe2 % 2) * 512 + 512],
                              start=f2, stop=l2).then_inc(s_av, 1)
                    for side in range(2):
                        pe.wait_ge(s_rv, 2 * gi2 + side + 1)
                        tgt = scoreA(pe2) if side == 0 else scoreB(pe2)
                        pe.matmul(tgt, ones1[0:1, :], rrow[0:1, 0:512],
                                  start=True, stop=True).then_inc(s_bc, 1)
                    pend = None
            if dbg in ("p2", "p2c", "p2p"):
                return
            pe.wait_ge(s_wc, 32)
            pe.wait_ge(s_ml, 64)
            pe.wait_ge(s_cp, 64)
            pe.wait_ge(s_exa, NEV)
            pe.wait_ge(s_exb, NEV)
            for qt in range(16):
                slot = [PSA, PSB][qt % 2][:, (qt // 2 % 2) * 1024:((qt // 2 % 2) + 1) * 1024]
                if qt >= 4:
                    pe.wait_ge(s_3v, qt - 3)
                for gp_ in range(8):
                    for n in range(2):
                        ins = pe.matmul(slot[:, n * 512:(n + 1) * 512],
                                        ctxf[:, gp_ * 2048 + qt * 128: gp_ * 2048 + (qt + 1) * 128],
                                        wob[:, gp_ * 1024 + n * 512: gp_ * 1024 + (n + 1) * 512],
                                        start=(gp_ == 0), stop=(gp_ == 7))
                ins.then_inc(s_3m, 1)
            if dbg == "p3":
                return
            n4m = 0
            for T in ((0,) if dbg in ("p4a", "p4b", "p4c", "p4f") else (0, 1)):
                pe.wait_ge(s_3v, 16)
                pe.wait_ge(s_4v, T * 40)
                pe.wait_ge(s_xT, (T + 1) * 16)
                for hb in range(8):
                    pe.wait_ge(s_wa[hb % 2], 32 * (T * 4 + hb // 2 + 1))
                    for hc in range(4):
                        hidc = hb * 4 + hc
                        rot = hidc % 2
                        pw1 = [PSA, PSB][rot][:, 0:1024]
                        pw2 = [PSA, PSB][rot][:, 1024:2048]
                        if hidc >= 2:
                            pe.wait_ge(s_4s, T * 96 + hidc - 1)
                            pe.wait_ge(s_4v, T * 40 + hidc - 1)
                        for w, pw in ((0, pw1), (1, pw2)):
                            for n in range(2):
                                for dinc in range(8):
                                    ins = pe.matmul(
                                        pw[:, n * 512:(n + 1) * 512],
                                        wfA[:, w, hb % 2, dinc, hc * 128:(hc + 1) * 128],
                                        xnT[:, dinc, n * 512:(n + 1) * 512],
                                        start=(dinc == 0), stop=(dinc == 7))
                            ins.then_inc(s_4m, 1)
                            n4m += 1
                if dbg == "p4a":
                    return
                pe.wait_ge(s_4s, T * 96 + 32)
                pe.wait_ge(s_4v, T * 40 + 32)
                for ho in range(32):
                    pe.wait_ge(s_wb[ho % 2], 16 * (T * 32 + ho // 2 + 1))
                    slot = [PSA, PSB][ho % 2][:, (ho // 2 % 2) * 1024:((ho // 2 % 2) + 1) * 1024]
                    if ho >= 4:
                        pe.wait_ge(s_4s, T * 96 + 32 + ho - 3)
                    for n in range(2):
                        for hinc in range(32):
                            ins = pe.matmul(slot[:, n * 512:(n + 1) * 512],
                                            wfB[:, ho % 2, hinc, :],
                                            hbuf[:, hinc, n * 512:(n + 1) * 512],
                                            start=(hinc == 0), stop=(hinc == 31))
                        ins.then_inc(s_4m, 1)
                        n4m += 1
                if dbg == "p4b":
                    return
                pe.wait_ge(s_4s, T * 96 + 64)
                for ho in range(32):
                    pe.wait_ge(s_wb[ho % 2], 16 * (T * 32 + 16 + ho // 2 + 1))
                    slot = [PSA, PSB][ho % 2][:, (ho // 2 % 2) * 1024:((ho // 2 % 2) + 1) * 1024]
                    if ho >= 4:
                        pe.wait_ge(s_4s, T * 96 + 64 + ho - 3)
                    for n in range(2):
                        for hinc in range(32):
                            ins = pe.matmul(slot[:, n * 512:(n + 1) * 512],
                                            wfB[:, ho % 2, hinc, :],
                                            h2b[:, hinc, n * 512:(n + 1) * 512],
                                            start=(hinc == 0), stop=(hinc == 31))
                        ins.then_inc(s_4m, 1)
                        n4m += 1
                if dbg == "p4c":
                    return
                pe.wait_ge(s_4s, T * 96 + 96)
                pe.wait_ge(s_w3, 64 * (T + 1))
                for qt in range(8):
                    gq = T * 8 + qt
                    slot = [PSA, PSB][qt % 2][:, (qt // 2 % 2) * 1024:((qt // 2 % 2) + 1) * 1024]
                    if qt >= 4:
                        pe.wait_ge(s_4v, d_done(T * 8 + qt - 4))
                    for n in range(2):
                        for hinc in range(32):
                            ins = pe.matmul(slot[:, n * 512:(n + 1) * 512],
                                            hbuf[:, hinc, qt * 128:(qt + 1) * 128],
                                            wf3[:, hinc, n * 512:(n + 1) * 512],
                                            start=(hinc == 0), stop=(hinc == 31))
                        ins.then_inc(s_4m, 1)
                        n4m += 1

        @blk.scalar
        def _(sc):
            for i in range(NT1):
                ip = i % 2
                base = ip * 8
                sc.wait_ge(s_x2[ip], 16 * (i // 2 + 1))
                sc.activation(sqs[:, :], xin[:, ip * 1024:(ip + 1) * 1024],
                              AF.Square, bias=0.0, scale=1.0,
                              accum_out=stt[:, base + 1:base + 2]).then_inc(s_se, 1)
                sc.wait_ge(s_vv, i + 1)
                sc.activation(stt[:, base + 3:base + 4], stt[:, base + 2:base + 3],
                              AF.Sqrt, bias=kc[:, 0:1], scale=1.0).then_inc(s_s2, 1)
            if dbg == "p1":
                return
            if dbg == "p2p":
                return
            n_ev = 0
            for hh in (0, 1):
                for gi in range(hh * 16, (hh + 1) * 16):
                    g = GQ[gi]
                    P = g["P"]
                    for ki, kt in enumerate(g["kts"]):
                        e = n_ev
                        sc.wait_ge(s_qk, e + 1)
                        if e >= 2:
                            sc.wait_ge(s_av, e - 1)
                        sA = PSA[:, (e % 2) * 1024:(e % 2) * 1024 + 512]
                        sB = PSA[:, (e % 2) * 1024 + 512:(e % 2 + 1) * 1024]
                        sc.activation(eT[:, (e % 2) * 512:(e % 2) * 512 + 512],
                                      sA, AF.Exp,
                                      bias=p2a[:, P * 32 + kt: P * 32 + kt + 1],
                                      scale=0.125).then_inc(s_exa, 1)
                        sc.activation(eT[:, 1024 + (e % 2) * 512: 1024 + (e % 2) * 512 + 512],
                                      sB, AF.Exp,
                                      bias=p2b[:, P * 32 + kt: P * 32 + kt + 1],
                                      scale=0.125).then_inc(s_exb, 1)
                        n_ev += 1
            if dbg in ("p2", "p2c", "p3"):
                return
            for T in ((0,) if dbg in ("p4a", "p4b", "p4c", "p4f") else (0, 1)):
                for hidc in range(32):
                    hb, hc = hidc // 4, hidc % 4
                    sc.wait_ge(s_4m, T * 208 + hb * 8 + hc * 2 + 1)
                    if hidc >= 2:
                        sc.wait_ge(s_4v, T * 40 + hidc - 1)
                    rot = hidc % 2
                    pw1 = [PSA, PSB][rot][:, 0:1024]
                    sc.activation(silu[:, rot, :], pw1, AF.Silu,
                                  bias=0.0, scale=cv[:, 16 + hidc:17 + hidc]).then_inc(s_4s, 1)
                if dbg == "p4a":
                    return
                for ho in range(32):
                    sc.wait_ge(s_4m, T * 208 + 64 + 2 * ho + 2)
                    slot = [PSA, PSB][ho % 2][:, (ho // 2 % 2) * 1024:((ho // 2 % 2) + 1) * 1024]
                    sc.activation(h2b[:, ho, :], slot, AF.Gelu,
                                  bias=0.0, scale=cv[:, 80 + ho:81 + ho]).then_inc(s_4s, 1)
                if dbg == "p4b":
                    return
                for ho in range(32):
                    sc.wait_ge(s_4m, T * 208 + 128 + 2 * ho + 2)
                    slot = [PSA, PSB][ho % 2][:, (ho // 2 % 2) * 1024:((ho // 2 % 2) + 1) * 1024]
                    sc.activation(hbuf[:, ho, :], slot, AF.Gelu,
                                  bias=0.0, scale=cv[:, 112 + ho:113 + ho]).then_inc(s_4s, 1)

        @blk.vector
        def _(ve):
            ve.memset(kc[:, 0:1], EPS)
            ve.memset(idn[:, :], 1.0).then_inc(s_v2, 1)
            ve.memset(ones1[0:1, :], 1.0).then_inc(s_v2, 1)
            ve.wait_ge(s_ge, 1)
            for P in range(8):
                hB = PAIRS[P][1]
                ve.tensor_scalar(p2b[:, P * 32:(P + 1) * 32], io32[:, :],
                                 SLOPES32[hB], -M_H[hB], ALU.mult, ALU.add)
            for P in (1, 2, 3, 4, 5, 6, 7, 0):
                hA = PAIRS[P][0]
                if P == 0:
                    ve.drain()
                ve.tensor_scalar(p2a[:, P * 32:(P + 1) * 32], io32[:, :],
                                 SLOPES32[hA], -M_H[hA], ALU.mult, ALU.add)
            for r in range(3):
                ve.wait_ge(s_pe, r + 1)
                tgt = [bvb, bob, s3b][r]
                ve.tensor_copy(tgt[:, 0:512], pB[:, :])
                ve.tensor_copy(tgt[:, 512:1024], pB2[:, :]).then_inc(s_v2, 1)
            for it in range(NT1 + 2):
                if it < NT1:
                    i = it
                    ip = i % 2
                    base = ip * 8
                    ve.wait_ge(s_x2[ip], 16 * (i // 2 + 1))
                    xi = xin[:, ip * 1024:(ip + 1) * 1024]
                    ve.reduce_sum(stt[:, base + 0:base + 1], xi, axis=AX.X)
                    ve.drain()
                    ve.tensor_scalar(stt[:, base + 5:base + 6], stt[:, base + 0:base + 1],
                                     1.0 / 1024, None, ALU.mult)
                    ve.drain()
                    ve.tensor_mul(stt[:, base + 6:base + 7], stt[:, base + 5:base + 6],
                                  stt[:, base + 5:base + 6])
                    ve.drain()
                    ve.wait_ge(s_se, i + 1)
                    ve.tensor_scalar(stt[:, base + 2:base + 3], stt[:, base + 1:base + 2],
                                     1.0 / 1024, stt[:, base + 6:base + 7],
                                     ALU.mult, ALU.subtract).then_inc(s_vv, 1)
                if 1 <= it <= NT1:
                    i = it - 1
                    ip = i % 2
                    base = ip * 8
                    xi = xin[:, ip * 1024:(ip + 1) * 1024]
                    ve.wait_ge(s_s2, i + 1)
                    ve.reciprocal(stt[:, base + 4:base + 5], stt[:, base + 3:base + 4])
                    ve.drain()
                    if i >= 2:
                        ve.wait_ge(s_pe, 3 + i - 1)
                    ve.tensor_scalar(xnt[:, ip * 1024:(ip + 1) * 1024], xi,
                                     stt[:, base + 5:base + 6], stt[:, base + 4:base + 5],
                                     ALU.subtract, ALU.mult).then_inc(s_ve, 1)
                if 2 <= it:
                    i = it - 2
                    ip = i % 2
                    ve.wait_ge(s_pe, 3 + i + 1)
                    g, j0 = i // 4, (i % 4) * 128
                    if j0 == 0 and g >= 2:
                        ve.wait_ge(s_sb[g % 2], 16 * ((g - 2) // 2 + 1))
                    ve.tensor_copy(xs[:, g % 2, :, j0:j0 + 128],
                                   pT[ip][:, :].rearrange("p (d j) -> p d j", d=8)).then_inc(s_v2, 1)
                    if dbg != "p1" and 16 <= i < 32:
                        j = i - 16
                        dc = j // 4
                        if j == 0:
                            ve.wait_ge(s_cv, 16)
                        ve.wait_ge(s_pp, j + 1)
                        ve.tensor_scalar(qf[:, dc * 2048 + (j % 4) * 512: dc * 2048 + (j % 4 + 1) * 512],
                                         pSC[j % 4], cv[:, 8 + dc: 9 + dc],
                                         None, ALU.add).then_inc(s_pk, 1)
            if dbg == "p1":
                return
            n_pp = 16
            n_pq = 0
            n_ev = 0
            n_gq = 0
            for hh in ((0,) if dbg == "p2p" else (0, 1)):
                if hh == 1:
                    ve.wait_ge(s_av, EV_HALF0_END)
                else:
                    ve.wait_ge(s_s2, NT1)
                    ve.wait_ge(s_sb[0], 16 * 6)
                    ve.wait_ge(s_sb[1], 16 * 6)
                ve.memset(vt[:, :, :, :].rearrange("p a b c -> p (a b c)"), 1.0)
                if hh == 1:
                    for dc in range(4):
                        for tc in range(4):
                            ve.wait_ge(s_pp, n_pp + 1)
                            ve.tensor_scalar(qf[:, dc * 2048 + tc * 512: dc * 2048 + (tc + 1) * 512],
                                             pSC[n_pp % 4], cv[:, 12 + dc: 13 + dc],
                                             None, ALU.add).then_inc(s_pk, 1)
                            n_pp += 1
                for b in range(4):
                    for dc in range(4):
                        for j2 in range(2):
                            ve.wait_ge(s_pp, n_pp + 1)
                            col = dc * 4096 + b * 1024 + j2 * 512
                            ve.tensor_scalar(kf[:, col:col + 512], pSC[n_pp % 4],
                                             cv[:, hh * 4 + dc: hh * 4 + dc + 1],
                                             None, ALU.add).then_inc(s_pk, 1)
                            n_pp += 1
                    for tt in range(8):
                        ve.wait_ge(s_pq, n_pq + 1)
                        vslot = vslot_of(n_pq)
                        tg = b * 8 + tt
                        ve.tensor_tensor(vt[:, tg, :, 0:64],
                                         vslot.rearrange("p (a c) -> p a c", a=8),
                                         bvb[:, hh * 512:(hh + 1) * 512]
                                         .rearrange("p (a c) -> p a c", a=8),
                                         ALU.add).then_inc(s_pv, 1)
                        n_pq += 1
                if dbg == "p2p":
                    continue
                for gi in range(hh * 16, (hh + 1) * 16):
                    g = GQ[gi]
                    P, pi, qc = g["P"], g["pi"], g["qc"]
                    n_ev += len(g["kts"])
                    eL = n_ev - 1
                    cA = PSB[:, (gi % 2) * 1024:(gi % 2) * 1024 + 512]
                    cB = PSB[:, (gi % 2) * 1024 + 512:(gi % 2 + 1) * 1024]
                    bA = PSA[:, (eL % 2) * 1024:(eL % 2) * 1024 + 512]
                    bB = PSA[:, (eL % 2) * 1024 + 512:(eL % 2 + 1) * 1024]
                    for side, pC, pBc in ((0, cA, bA), (1, cB, bB)):
                        ve.wait_ge(s_av, n_ev)
                        ve.wait_ge(s_bc, 2 * gi + side)
                        ve.reciprocal(rrow[0:1, 0:512], pC[64:65, :]).then_inc(s_rv, 1)
                        ve.wait_ge(s_bc, 2 * gi + side + 1)
                        ve.tensor_copy(rbc[0:64, 0:512], pBc[0:64, :]).then_inc(s_cp, 1)
                        ve.drain()
                        gp_ = 4 * hh + pi
                        ve.tensor_tensor(
                            ctxf[side * 64:(side + 1) * 64,
                                 gp_ * 2048 + qc * 512: gp_ * 2048 + (qc + 1) * 512],
                            pC[0:64, :], rbc[0:64, 0:512], ALU.mult).then_inc(s_ml, 1)
                    n_gq += 1
            if dbg in ("p2", "p2c", "p2p"):
                return
            for qt in range(16):
                slot = [PSA, PSB][qt % 2][:, (qt // 2 % 2) * 1024:((qt // 2 % 2) + 1) * 1024]
                ve.wait_ge(s_3m, qt + 1)
                ve.wait_ge(s_3x2[qt % 2], 16 * (qt // 2 + 1))
                if qt >= 2:
                    ve.wait_ge(s_3a2[qt % 2], 16 * ((qt - 2) // 2 + 1))
                ao = aout[:, (qt % 2) * 1024:(qt % 2 + 1) * 1024]
                ve.tensor_tensor(ao, slot, bob[:, :], ALU.add)
                ve.drain()
                ve.tensor_tensor(ao, ao, xacc[:, (qt % 2) * 1024:(qt % 2 + 1) * 1024],
                                 ALU.add).then_inc(s_3v, 1)
            if dbg == "p3":
                return
            for T in ((0,) if dbg in ("p4a", "p4b", "p4c", "p4f") else (0, 1)):
                for hidc in range(32):
                    hb, hc = hidc // 4, hidc % 4
                    if T == 0 and hidc == 0:
                        ve.wait_ge(s_3a2[0], 16 * 8)
                        ve.wait_ge(s_3a2[1], 16 * 8)
                    ve.wait_ge(s_4m, T * 208 + hb * 8 + hc * 2 + 2)
                    ve.wait_ge(s_4s, T * 96 + hidc + 1)
                    rot = hidc % 2
                    pw2 = [PSA, PSB][rot][:, 1024:2048]
                    ve.scalar_tensor_tensor(hbuf[:, hidc, :], pw2,
                                            cv[:, 48 + hidc:49 + hidc],
                                            silu[:, rot, :],
                                            ALU.mult, ALU.mult).then_inc(s_4v, 1)
                if dbg in ("p4a", "p4b", "p4c"):
                    return
                for qt in range(8):
                    gq = T * 8 + qt
                    slot = [PSA, PSB][qt % 2][:, (qt // 2 % 2) * 1024:((qt // 2 % 2) + 1) * 1024]
                    ve.wait_ge(s_4m, T * 208 + 192 + 2 * qt + 2)
                    ve.wait_ge(s_ac[gq % 2], 16 * (gq // 2 + 1))
                    if gq >= 2:
                        ve.wait_ge(s_4o2[gq % 2], 16 * ((gq - 2) // 2 + 1))
                    uo = uout[:, gq % 2, :]
                    ve.tensor_tensor(uo, slot, s3b[:, :], ALU.mult)
                    ve.drain()
                    ve.tensor_tensor(uo, uo, acci[:, gq % 2, :], ALU.add).then_inc(s_4v, 1)


    es.close()
    return nc


def make_inputs(ins):
    x = np.asarray(ins["L0_attn_inputs"], dtype=np.float32)
    f32 = np.float32
    cvec = np.zeros((128, 144), dtype=f32)
    cvec[:, 0:8] = np.asarray(ins["L0_attn_bk"], f32).reshape(8, 128).T
    cvec[:, 8:16] = np.asarray(ins["L0_attn_bq"], f32).reshape(8, 128).T
    cvec[:, 16:48] = np.asarray(ins["L0_ffn_w1_s"], f32).reshape(32, 128).T
    cvec[:, 48:80] = np.asarray(ins["L0_ffn_w2_s"], f32).reshape(32, 128).T
    cvec[:, 80:112] = np.asarray(ins["L0_ffn_h1_s"], f32).reshape(32, 128).T
    cvec[:, 112:144] = np.asarray(ins["L0_ffn_h2_s"], f32).reshape(32, 128).T
    cvec2 = np.concatenate([np.asarray(ins["L0_attn_bv"], f32),
                            np.asarray(ins["L0_attn_bo"], f32),
                            np.asarray(ins["L0_ffn_w3_s"], f32)]).reshape(1, 3072)

    def slab_a(w):
        wT = np.ascontiguousarray(np.asarray(w, dtype=f32).T)
        return np.ascontiguousarray(wT.reshape(8, 128, 8, 512).transpose(2, 0, 1, 3))
    def slab_h(w):
        ww = np.asarray(w, dtype=f32)
        return np.ascontiguousarray(ww.reshape(32, 128, 32, 128).transpose(0, 3, 2, 1))
    def slab_3(w):
        wT = np.ascontiguousarray(np.asarray(w, dtype=f32).T)
        return np.ascontiguousarray(wT.reshape(32, 128, 1024))

    shared = {
        "Wq": np.ascontiguousarray(np.asarray(ins["L0_attn_Wq"], f32)),
        "Wk": np.ascontiguousarray(np.asarray(ins["L0_attn_Wk"], f32)),
        "Wv": np.ascontiguousarray(np.asarray(ins["L0_attn_Wv"], f32)),
        "Wo": np.ascontiguousarray(np.asarray(ins["L0_attn_Wo"], f32)),
        "cvec": cvec, "cvec2": cvec2,
        "w1sl": slab_a(ins["L0_ffn_w1_w"]), "w2sl": slab_a(ins["L0_ffn_w2_w"]),
        "h1sl": slab_h(ins["L0_ffn_h1_w"]), "h2sl": slab_h(ins["L0_ffn_h2_w"]),
        "w3sl": slab_3(ins["L0_ffn_w3_w"]),
    }
    in_maps = []
    p = np.arange(128)[:, None]
    for c in range(8):
        b, qbase = c // 2, (c % 2) * QC
        m = dict(shared)
        m["x_seq"] = np.ascontiguousarray(x[b])
        m["x_own"] = np.ascontiguousarray(x[b, qbase:qbase + QC])
        for h in range(0, H, 2):
            j = np.arange(UW[h])[None, :]
            dgrid = p + (3968 - qbase) - j
            T = (np.float32(SLOPES32[h]) * dgrid.astype(f32)).astype(f32)
            eps8 = 8.0 * (T.astype(np.float64) - np.float64(SLOPES32[h]) * dgrid)
            m[f"U{h}"] = eps8.astype(f32)
        in_maps.append(m)
    return in_maps


def kernel(**inputs):
    in_maps = make_inputs(inputs)
    nc = build_nc()
    res = run_bass_kernel_spmd(nc, in_maps, core_ids=list(range(8)))
    o = np.empty((B, S, E), dtype=np.float32)
    for c in range(8):
        b, qbase = c // 2, (c % 2) * QC
        o[b, qbase:qbase + QC] = res.results[c]["out"]
    return o


# version 08 (final)
# speedup vs baseline: 1.1348x; 1.1348x over previous
import math
from contextlib import ExitStack

import numpy as np
import ml_dtypes
import concourse.bass as bass
import concourse.mybir as mybir
from concourse.bass_utils import run_bass_kernel_spmd

F32 = mybir.dt.float32
BF16 = mybir.dt.bfloat16
AF = mybir.ActivationFunctionType
ALU = mybir.AluOpType
AX = mybir.AxisListType

B, S, E, H, D = 4, 4096, 1024, 16, 64
DH, QC, NKT = 4096, 2048, 32
THRESH = 36.0
EPS = 1e-5

_xb = (2.0 ** 8) ** (1.0 / H)
SLOPES64 = [1.0 / _xb ** (i + 1) for i in range(H)]
SLOPES32 = [float(np.float32(s)) for s in SLOPES64]

def _kt0(h):
    s = SLOPES32[h]
    lo = 3968.0 - THRESH / s
    return 0 if lo <= 0 else min(NKT - 1, int(math.ceil(lo / 128.0)))

KT0 = [_kt0(h) for h in range(H)]
PAIRS = [(2 * i, 2 * i + 1) for i in range(H // 2)]
PAIR_KT0 = [min(KT0[a], KT0[b]) for a, b in PAIRS]
UW = {h: 6016 - PAIR_KT0[h // 2] * 128 for h in range(0, H, 2)}
M_H = [SLOPES32[h] * 4095.0 for h in range(H)]

NT1 = 48
LD0 = 16
NQK = 48
NVG = 32

GQ = []
for _hh in (0, 1):
    for _pi in range(4):
        _P = 4 * _hh + _pi
        for _qc in range(4):
            GQ.append(dict(hh=_hh, pi=_pi, P=_P, hA=PAIRS[_P][0], hB=PAIRS[_P][1],
                           qc=_qc, kts=list(range(PAIR_KT0[_P], NKT))))
EV_START = {}
_e = 0
for _gi, _g in enumerate(GQ):
    EV_START[_gi] = _e
    _e += len(_g["kts"])
NEV = _e
NGRP = len(GQ)
EV_HALF0_END = EV_START[16]
PAIR_EV_START = {P: EV_START[4 * P] for P in range(8)}


def build_nc(dbg=None):
    nc = bass.Bass("TRN2", target_bir_lowering=False, debug=False)
    es = ExitStack()

    x_seq = nc.dram_tensor("x_seq", [S, E], F32, kind="ExternalInput").ap()
    x_own = nc.dram_tensor("x_own", [QC, E], F32, kind="ExternalInput").ap()
    WqD = nc.dram_tensor("Wq", [E, E], F32, kind="ExternalInput").ap()
    WkD = nc.dram_tensor("Wk", [E, E], F32, kind="ExternalInput").ap()
    WvD = nc.dram_tensor("Wv", [E, E], F32, kind="ExternalInput").ap()
    WoD = nc.dram_tensor("Wo", [E, E], F32, kind="ExternalInput").ap()
    cvec = nc.dram_tensor("cvec", [128, 144], F32, kind="ExternalInput").ap()
    cvec2 = nc.dram_tensor("cvec2", [1, 3072], F32, kind="ExternalInput").ap()
    UtD = {h: nc.dram_tensor(f"U{h}", [128, UW[h]], F32, kind="ExternalInput").ap()
           for h in range(0, H, 2)}
    w1sl = nc.dram_tensor("w1sl", [8, 8, 128, 512], F32, kind="ExternalInput").ap()
    w2sl = nc.dram_tensor("w2sl", [8, 8, 128, 512], F32, kind="ExternalInput").ap()
    h1sl = nc.dram_tensor("h1sl", [32, 128, 32, 128], F32, kind="ExternalInput").ap()
    h2sl = nc.dram_tensor("h2sl", [32, 128, 32, 128], F32, kind="ExternalInput").ap()
    w3sl = nc.dram_tensor("w3sl", [32, 128, 1024], F32, kind="ExternalInput").ap()
    out = nc.dram_tensor("out", [QC, E], F32, kind="ExternalOutput").ap()
    xnf_d = nc.dram_tensor("xnf_d", [8, 128, S], BF16, kind="Internal").ap()
    xno_d = nc.dram_tensor("xno_d", [8, 128, QC], BF16, kind="Internal").ap()
    acc_d = nc.dram_tensor("acc_d", [16, 128, E], F32, kind="Internal").ap()
    dbg_outs = {}
    if dbg == "p1":
        dbg_outs["xnf_o"] = nc.dram_tensor("xnf_o", [8, 128, S], BF16, kind="ExternalOutput").ap()
        dbg_outs["xno_o"] = nc.dram_tensor("xno_o", [8, 128, QC], BF16, kind="ExternalOutput").ap()
    if dbg in ("p2", "p2p"):
        dbg_outs["kf_o"] = nc.dram_tensor("kf_o", [128, 16384], BF16, kind="ExternalOutput").ap()
        dbg_outs["vt_o"] = nc.dram_tensor("vt_o", [128, 16640], BF16, kind="ExternalOutput").ap()
        dbg_outs["qf_o"] = nc.dram_tensor("qf_o", [128, 8192], BF16, kind="ExternalOutput").ap()
    if dbg in ("p2", "p2c"):
        dbg_outs["ctx_o"] = nc.dram_tensor("ctx_o", [128, 16384], BF16, kind="ExternalOutput").ap()
    if dbg == "p3":
        dbg_outs["acc_o"] = nc.dram_tensor("acc_o", [16, 128, E], F32, kind="ExternalOutput").ap()
    if dbg in ("p4a", "p4b", "p4c"):
        dbg_outs["hb_o"] = nc.dram_tensor("hb_o", [128, 32768], BF16, kind="ExternalOutput").ap()
    if dbg == "p4d":
        dbg_outs["acc_o"] = nc.dram_tensor("acc_o", [16, 128, E], F32, kind="ExternalOutput").ap()

    sbA = lambda *a: nc.alloc_sbuf_tensor(*a).ap()
    sem = lambda n: es.enter_context(nc.semaphore(n))

    cv = sbA("cv", [128, 144], F32)
    idn = sbA("idn", [128, 128], F32)
    p2b = sbA("p2b", [128, 256], F32)
    p2a = sbA("p2a", [128, 256], F32)
    bvb = sbA("bvb", [128, 1024], BF16)
    bob = sbA("bob", [128, 1024], BF16)
    s3b = sbA("s3b", [128, 1024], BF16)
    stt = sbA("stt", [128, 16], F32)
    idnb = sbA("idnb", [128, 128], BF16)
    ones1 = sbA("ones1", [1, 128], F32)
    io32 = p2a[:, 0:32]
    kc = stt[:, 7:8]

    AR16 = nc.alloc_sbuf_tensor("AR16", [128, 84224], BF16).ap()
    AR32 = nc.alloc_sbuf_tensor("AR32", [128, 8192], F32).ap()

    def at(name, shape, dt, off):
        ds = 2 if dt == BF16 else 4
        base = AR16 if dt == BF16 else AR32
        nel = 1
        for s in shape[1:]:
            nel *= s
        assert off % ds == 0, (name, off)
        v = base[:, off // ds: off // ds + nel]
        if len(shape) == 3:
            v = v.rearrange("p (a b) -> p a b", a=shape[1])
        elif len(shape) == 4:
            v = v.rearrange("p (a b c) -> p a b c", a=shape[1], b=shape[2])
        elif len(shape) == 5:
            v = v.rearrange("p (a b c d) -> p a b c d", a=shape[1], b=shape[2],
                            c=shape[3])
        return v
    xin = at("xin", [128, 2048], F32, 0)
    xnt = at("xnt", [128, 2048], F32, 8192)
    xs = at("xs", [128, 2, 8, 512], BF16, 0)
    sqs = at("sqs", [128, 1024], F32, 16384)
    kf = at("kf", [128, 16384], BF16, 0)
    vt = at("vt", [128, 32, 8, 65], BF16, 32768)
    qf = at("qf", [128, 8192], BF16, 66048)
    ub = at("ub", [128, 4480], F32, 8192)
    tsc = at("tsc", [128, 2, 512], F32, 26112)
    c2 = at("c2", [1, 3072], F32, 20480)
    eT = at("eT", [128, 2048], BF16, 98816)
    
    rrow = at("rrow", [1, 1024], F32, 0)
    rbc = at("rbc", [128, 1024], F32, 4096)
    ctxf = at("ctxf", [128, 16384], BF16, 102912)
    xnfb = at("xnfb", [128, 2, 8, 1024], BF16, 135680)
    xnob = at("xnob", [128, 8, 2048], BF16, 135680)
    wqb = at("wqb", [128, 4096], BF16, 16384)
    wkb = at("wkb", [128, 4096], BF16, 82432)
    wvb = at("wvb", [128, 4096], BF16, 90624)
    wob = at("wob", [128, 8192], BF16, 135680)
    xacc = at("xacc", [128, 2048], F32, 0)
    aout = at("aout", [128, 2048], F32, 8192)
    hbuf = at("hbuf", [128, 32, 1024], BF16, 0)
    h2b = at("h2b", [128, 32, 1024], BF16, 65536)
    wfA = at("wfA", [128, 2, 2, 8, 512], BF16, 65536)
    wf3 = at("wf3", [128, 32, 1024], BF16, 65536)
    xnT = at("xnT", [128, 8, 1024], BF16, 147456)
    wfB = at("wfB", [128, 2, 32, 128], BF16, 131072)
    silu = at("silu", [128, 2, 1024], BF16, 163840)
    acci = at("acci", [128, 2, 1024], F32, 16384)
    uout = at("uout", [128, 2, 1024], F32, 24576)
    wstg = at("wstg", [128, 2, 2, 512], F32, 20480)

    PSA = nc.alloc_psum_tensor("PSA", [128, 2048], F32).ap()
    PSB = nc.alloc_psum_tensor("PSB", [128, 2048], F32).ap()
    pT = [PSB[:, 0:1024], PSB[:, 1024:2048]]
    pB, pB2 = PSA[:, 0:512], PSA[:, 512:1024]
    pSC = [PSA[:, j * 512:(j + 1) * 512] for j in range(4)]
    pCA = PSB[:, 0:1024]
    pCB = PSB[:, 1024:2048]

    s_ld = sem("s_ld"); s_lc = sem("s_lc"); s_lb = sem("s_lb"); s_ub = sem("s_ub")
    s_st = sem("s_st"); s_ge = sem("s_ge"); s_wl = sem("s_wl")
    s_pe = sem("s_pe"); s_se = sem("s_se"); s_s2 = sem("s_s2")
    s_ve = sem("s_ve"); s_vv = sem("s_vv"); s_v2 = sem("s_v2")
    s_pp = sem("s_pp"); s_pq = sem("s_pq"); s_pk = sem("s_pk"); s_pv = sem("s_pv")
    s_qk = sem("s_qk"); s_vst = sem("s_vst"); s_exa = sem("s_exa"); s_exb = sem("s_exb")
    s_av = sem("s_av"); s_rv = sem("s_rv"); s_bc = sem("s_bc"); s_cp = sem("s_cp")
    s_ml = sem("s_ml")
    s_3m = sem("s_3m"); s_3v = sem("s_3v"); s_3x = sem("s_3x"); s_3a = sem("s_3a")
    s_4w = sem("s_4w"); s_4x = sem("s_4x"); s_4m = sem("s_4m"); s_4v = sem("s_4v")
    s_4s = sem("s_4s"); s_4o = sem("s_4o"); s_4a = sem("s_4a")
    s_cv = sem("s_cv")
    s_x2 = [sem("s_x0"), sem("s_x1")]
    s_sb = [sem("s_sb0"), sem("s_sb1")]
    s_xf = [sem("s_xf0"), sem("s_xf1")]
    s_3x2 = [sem("s_3x0"), sem("s_3x1")]
    s_3a2 = [sem("s_3a0"), sem("s_3a1")]
    s_wa = [sem("s_wa0"), sem("s_wa1")]
    s_wc = sem("s_wc"); s_ws = [sem("s_ws0"), sem("s_ws1")]
    s_wb = [sem("s_wb0"), sem("s_wb1")]
    s_w3 = sem("s_w3")
    s_xT = sem("s_xT")
    s_ac = [sem("s_ac0"), sem("s_ac1")]
    s_4o2 = [sem("s_4o0"), sem("s_4o1")]

    def vslot_of(n):
        return [pCA, pCB][n % 2][:, (n % 4 // 2) * 512:(n % 4 // 2) * 512 + 512]

    def d_done(g):
        tt, qq = divmod(g, 8)
        return tt * 40 + 32 + qq + 1

    with nc.Block() as blk:

        @blk.sync
        def _(sy):
            sy.dma_start(cv[:, :], cvec).then_inc(s_cv, 16)
            sy.dma_start(c2[0:1, :], cvec2).then_inc(s_lc, 16)
            for i in range(NT1):
                ip = i % 2
                if i >= 2:
                    sy.wait_ge(s_ve, i - 1)
                    sy.wait_ge(s_se, i - 1)
                src = x_own[i * 128:(i + 1) * 128, :] if i < 16 else \
                      x_seq[(i - 16) * 128:(i - 15) * 128, :]
                sy.dma_start(xin[:, ip * 1024:(ip + 1) * 1024], src).then_inc(s_x2[ip], 16)
                if i >= 5 and (i - 5) % 4 == 0:
                    g = (i - 5) // 4
                    sy.wait_ge(s_v2, 5 + 4 * g + 4)
                    dst = xno_d[:, :, 512 * g:512 * (g + 1)] if g < 4 else \
                          xnf_d[:, :, 512 * (g - 4):512 * (g - 3)]
                    sy.dma_start(dst.rearrange("d p j -> p d j"),
                                 xs[:, g % 2, :, :]).then_inc(s_sb[g % 2], 16)
                    if g == 3 and dbg != "p1":
                        sy.wait_ge(s_sb[0], 32)
                        sy.wait_ge(s_sb[1], 32)
                        sy.dma_start(xnob[:, :, :],
                                     xno_d.rearrange("d p j -> p d j")).then_inc(s_lb, 16)
            for g in (11,):
                sy.wait_ge(s_v2, 5 + 4 * g + 4)
                sy.dma_start(xnf_d[:, :, 512 * (g - 4):512 * (g - 3)]
                             .rearrange("d p j -> p d j"),
                             xs[:, g % 2, :, :]).then_inc(s_sb[g % 2], 16)
            if dbg == "p1":
                sy.wait_ge(s_sb[0], 16 * 6)
                sy.wait_ge(s_sb[1], 16 * 6)
                sy.dma_start(dbg_outs["xnf_o"], xnf_d).then_inc(s_st, 16)
                sy.dma_start(dbg_outs["xno_o"], xno_d).then_inc(s_st, 16)
                sy.wait_ge(s_st, 16 * 2)
                return
            for hh in ((0,) if dbg == "p2p" else (0, 1)):
                if hh == 1:
                    sy.wait_ge(s_pp, NQK)
                    sy.wait_ge(s_pq, NVG)
                    sy.dma_start(xnob[:, :, :],
                                 xno_d.rearrange("d p j -> p d j")).then_inc(s_lb, 16)
                for b in range(4):
                    gidx = hh * 4 + b
                    sy.wait_ge(s_sb[0], 16 * (b + 3))
                    sy.wait_ge(s_sb[1], 16 * (b + 3))
                    sy.wait_ge(s_pp, hh * NQK + 16)
                    if gidx >= 2:
                        hh2, b2 = divmod(gidx - 2, 4)
                        sy.wait_ge(s_pp, hh2 * NQK + 16 + (b2 + 1) * 8)
                        sy.wait_ge(s_pq, hh2 * NVG + (b2 + 1) * 8)
                    sy.dma_start(xnfb[:, b % 2, :, :],
                                 xnf_d[:, :, b * 1024:(b + 1) * 1024]
                                 .rearrange("d p j -> p d j")).then_inc(s_xf[b % 2], 16)
                if dbg == "p2p":
                    continue
                for gi in range(hh * 16, (hh + 1) * 16):
                    g = GQ[gi]
                    hA, qc = g["hA"], g["qc"]
                    Wp = (31 - PAIR_KT0[g["P"]]) * 128 + 512
                    if gi > 0:
                        sy.wait_ge(s_vst, EV_START[gi])
                    if gi == hh * 16:
                        sy.wait_ge(s_pp, (hh + 1) * NQK)
                        sy.wait_ge(s_pq, (hh + 1) * NVG)
                    sy.dma_start(ub[:, 0:Wp],
                                 UtD[hA][:, qc * 512: qc * 512 + Wp]).then_inc(s_ub, 16)
            if dbg == "p2p":
                sy.wait_ge(s_pk, NQK)
                sy.wait_ge(s_pv, NVG)
                sy.dma_start(dbg_outs["kf_o"], kf[:, :]).then_inc(s_st, 16)
                sy.dma_start(dbg_outs["vt_o"],
                             vt[:, :, :, :].rearrange("p a b c -> p (a b c)")).then_inc(s_st, 16)
                sy.dma_start(dbg_outs["qf_o"], qf[:, :]).then_inc(s_st, 16)
                sy.wait_ge(s_st, 16 * 3)
                return
            if dbg == "p2":
                sy.wait_ge(s_pk, 2 * NQK)
                sy.wait_ge(s_pv, 2 * NVG)
                sy.dma_start(dbg_outs["kf_o"], kf[:, :]).then_inc(s_st, 16)
                sy.dma_start(dbg_outs["vt_o"],
                             vt[:, :, :, :].rearrange("p a b c -> p (a b c)")).then_inc(s_st, 16)
                sy.dma_start(dbg_outs["qf_o"], qf[:, :]).then_inc(s_st, 16)
            if dbg in ("p2", "p2c"):
                sy.wait_ge(s_ml, 64)
                sy.dma_start(dbg_outs["ctx_o"], ctxf[:, :]).then_inc(s_st, 16)
                extra = 4 if dbg == "p2" else 1
                sy.wait_ge(s_st, 16 * extra)
                return
            for qt in range(16):
                if qt == 0:
                    sy.wait_ge(s_qk, NEV)
                    sy.wait_ge(s_ml, 2 * NGRP)
                if qt >= 2:
                    sy.wait_ge(s_3v, qt - 1)
                sy.dma_start(xacc[:, (qt % 2) * 1024:(qt % 2 + 1) * 1024],
                             x_own[qt * 128:(qt + 1) * 128, :]).then_inc(s_3x2[qt % 2], 16)
                sy.wait_ge(s_3v, qt + 1)
                sy.dma_start(acc_d[qt], aout[:, (qt % 2) * 1024:(qt % 2 + 1) * 1024]
                             ).then_inc(s_3a2[qt % 2], 16)
            if dbg == "p3":
                sy.wait_ge(s_3a2[0], 16 * 8)
                sy.wait_ge(s_3a2[1], 16 * 8)
                sy.dma_start(dbg_outs["acc_o"], acc_d).then_inc(s_st, 16)
                sy.wait_ge(s_st, 16)
                return
            if dbg in ("p4a", "p4b", "p4c"):
                sy.wait_ge(s_3m, 16)
                sy.wait_ge(s_3a2[0], 16 * 8)
                sy.wait_ge(s_3a2[1], 16 * 8)
                sy.dma_start(xnT[:, :, :],
                             xno_d[:, :, 0:1024].rearrange("d p j -> p d j")).then_inc(s_xT, 16)
                if dbg == "p4a":
                    sy.wait_ge(s_4v, 32)
                    sy.dma_start(dbg_outs["hb_o"], hbuf[:, :, :].rearrange("p a b -> p (a b)")).then_inc(s_st, 16)
                elif dbg == "p4b":
                    sy.wait_ge(s_4s, 64)
                    sy.dma_start(dbg_outs["hb_o"], h2b[:, :, :].rearrange("p a b -> p (a b)")).then_inc(s_st, 16)
                else:
                    sy.wait_ge(s_4s, 96)
                    sy.dma_start(dbg_outs["hb_o"], hbuf[:, :, :].rearrange("p a b -> p (a b)")).then_inc(s_st, 16)
                sy.wait_ge(s_st, 16)
                return
            n4x = 0
            n4o = 0
            for T in ((0,) if dbg == "p4f" else (0, 1)):
                if T == 0:
                    sy.wait_ge(s_3m, 16)
                    sy.wait_ge(s_3a2[0], 16 * 8)
                    sy.wait_ge(s_3a2[1], 16 * 8)
                else:
                    sy.wait_ge(s_4m, 64)
                sy.dma_start(xnT[:, :, :],
                             xno_d[:, :, T * 1024:(T + 1) * 1024]
                             .rearrange("d p j -> p d j")).then_inc(s_xT, 16)
                n4x += 1
                for qt in range(8):
                    gq = T * 8 + qt
                    if gq >= 2:
                        sy.wait_ge(s_4v, d_done(gq - 2))
                    sy.dma_start(acci[:, gq % 2, :], acc_d[gq]).then_inc(s_ac[gq % 2], 16)
                    n4x += 1
                    sy.wait_ge(s_4v, d_done(gq))
                    sy.dma_start(out[(T * 8 + qt) * 128:(T * 8 + qt + 1) * 128, :],
                                 uout[:, gq % 2, :]).then_inc(s_4o2[gq % 2], 16)
            sy.wait_ge(s_4o2[0], 16 * (4 if dbg == "p4f" else 8))
            sy.wait_ge(s_4o2[1], 16 * (4 if dbg == "p4f" else 8))
            if dbg == "p4d":
                sy.dma_start(dbg_outs["acc_o"], acc_d).then_inc(s_st, 16)
                sy.wait_ge(s_st, 16)

        @blk.gpsimd
        def _(gp):
            gp.iota(io32[:, :], [[128, 32]], base=0, channel_multiplier=1,
                    allow_small_or_imprecise_dtypes=True).then_inc(s_ge, 1)
            gp.wait_ge(s_v2, 1)
            gp.affine_select(idn[:, :], idn[:, :], [[-1, 128]], ALU.is_equal,
                             0.0, base=0, channel_multiplier=1)
            gp.drain()
            gp.tensor_copy(idnb[:, :], idn[:, :]).then_inc(s_ge, 2)
            if dbg == "p1":
                return
            n_ws = [0, 0]
            n_wc = [0]

            def wload(WD, dstb, colr, prewait=None):
                ncol = colr[1] - colr[0]
                dpc = 512 // ncol * 2 if ncol <= 512 else 1
                nch = 8 // dpc
                for ch in range(nch):
                    sl = ch % 2
                    if prewait is not None and ch == 0:
                        prewait()
                    if n_wc[0] >= 2:
                        gp.wait_ge(s_wc, n_wc[0] - 1)
                    srcw = WD[128 * dpc * ch:128 * dpc * (ch + 1), colr[0]:colr[1]] \
                        .rearrange("(d p) j -> p d j", p=128)
                    wsf = wstg[:, sl, :, :].rearrange("p a b -> p (a b)") \
                        .rearrange("p (d j) -> p d j", j=ncol)[:, 0:dpc, :]
                    gp.dma_start(wsf, srcw).then_inc(s_ws[sl], 16)
                    n_ws[sl] += 1
                    gp.wait_ge(s_ws[sl], 16 * n_ws[sl])
                    gp.tensor_copy(dstb[:, :].rearrange("p (d j) -> p d j", d=8)
                                   [:, dpc * ch:dpc * (ch + 1), :], wsf).then_inc(s_wc, 1)
                    n_wc[0] += 1

            for hh in ((0,) if dbg == "p2p" else (0, 1)):
                if hh == 1:
                    gp.wait_ge(s_qk, EV_HALF0_END)
                    gp.wait_ge(s_vst, EV_HALF0_END)
                    gp.wait_ge(s_exa, EV_HALF0_END)
                else:
                    gp.wait_ge(s_pe, 3)
                for WD, dstb in ((WqD, wqb), (WkD, wkb), (WvD, wvb)):
                    wload(WD, dstb, (hh * 512, (hh + 1) * 512))
            if dbg in ("p2", "p2c", "p2p"):
                return
            def _wo_pre():
                gp.wait_ge(s_pp, 2 * NQK)
                gp.wait_ge(s_pq, 2 * NVG)
                gp.wait_ge(s_vst, NEV)
                gp.wait_ge(s_exa, NEV)
            wload(WoD, wob, (0, 1024), prewait=_wo_pre)
            if dbg == "p3":
                return
            n4w = 0
            for T in ((0,) if dbg in ("p4a", "p4b", "p4c", "p4f") else (0, 1)):
                for hb in range(8):
                    if T == 0 and hb == 0:
                        gp.wait_ge(s_av, NEV)
                        gp.wait_ge(s_ml, 64)
                        gp.wait_ge(s_3m, 16)
                    if hb >= 2:
                        gp.wait_ge(s_4m, T * 208 + (hb - 1) * 8)
                    elif T > 0:
                        gp.wait_ge(s_4m, T * 208)
                    for w, sl in ((0, w1sl), (1, w2sl)):
                        gp.dma_start(wfA[:, w, hb % 2, :, :],
                                     sl[hb].rearrange("d p j -> p d j")).then_inc(s_wa[hb % 2], 16)
                        n4w += 1
                if dbg == "p4a":
                    return
                for ho in range(32):
                    if T == 0 and ho == 0:
                        pass
                    if ho >= 2:
                        gp.wait_ge(s_4m, T * 208 + 64 + (ho - 1) * 2)
                    elif T > 0:
                        gp.wait_ge(s_4m, (T - 1) * 208 + 192)
                    else:
                        gp.wait_ge(s_3m, 16)
                    gp.dma_start(wfB[:, ho % 2, :, :], h1sl[ho]).then_inc(s_wb[ho % 2], 16)
                    n4w += 1
                if dbg == "p4b":
                    return
                for ho in range(32):
                    if ho >= 2:
                        gp.wait_ge(s_4m, T * 208 + 128 + (ho - 1) * 2)
                    else:
                        gp.wait_ge(s_4m, T * 208 + 128)
                    gp.dma_start(wfB[:, ho % 2, :, :], h2sl[ho]).then_inc(s_wb[ho % 2], 16)
                    n4w += 1
                if dbg == "p4c":
                    return
                gp.wait_ge(s_4m, T * 208 + 192)
                for wq in range(4):
                    gp.dma_start(wf3[:, wq * 8:(wq + 1) * 8, :],
                                 w3sl[wq * 8:(wq + 1) * 8].rearrange("h p j -> p h j")
                                 ).then_inc(s_w3, 16)
                    n4w += 1

        @blk.tensor
        def _(pe):
            pe.wait_ge(s_v2, 2)
            pe.wait_ge(s_lc, 16)
            for r in range(3):
                if r > 0:
                    pe.wait_ge(s_v2, 2 + r)
                pe.matmul(pB[:, :], ones1[0:1, :],
                          c2[0:1, r * 1024: r * 1024 + 512], start=True, stop=True)
                pe.matmul(pB2[:, :], ones1[0:1, :],
                          c2[0:1, r * 1024 + 512: (r + 1) * 1024],
                          start=True, stop=True).then_inc(s_pe, 1)
            pe.wait_ge(s_ge, 3)
            n_pp_box = [0]

            def q_proj(hh, lo, hi):
                n_pp = n_pp_box[0]
                if lo == 0:
                    pe.wait_ge(s_wc, hh * 12 + 4)
                    pe.wait_ge(s_lb, (hh + 1) * 16)
                    if hh == 0:
                        pe.wait_ge(s_v2, 5)
                    else:
                        pe.wait_ge(s_ml, 32)
                        pe.wait_ge(s_cp, 32)
                        pe.wait_ge(s_exa, EV_HALF0_END)
                        pe.wait_ge(s_exb, EV_HALF0_END)
                for g in range(lo, hi):
                    dc, tc = g // 4, g % 4
                    slot = pSC[n_pp % 4]
                    if n_pp >= 4:
                        pe.wait_ge(s_pk, n_pp - 3)
                    for dinc in range(8):
                        ins = pe.matmul(slot,
                                        wqb[:, dinc * 512 + dc * 128: dinc * 512 + (dc + 1) * 128],
                                        xnob[:, dinc, tc * 512:(tc + 1) * 512],
                                        start=(dinc == 0), stop=(dinc == 7))
                    ins.then_inc(s_pp, 1)
                    n_pp += 1
                n_pp_box[0] = n_pp

            for i in range(NT1):
                ip = i % 2
                pe.wait_ge(s_ve, i + 1)
                if i >= 2:
                    pe.wait_ge(s_v2, 5 + i - 1)
                for dc in range(8):
                    ins = pe.transpose(pT[ip][:, dc * 128:(dc + 1) * 128],
                                       xnt[:, ip * 1024 + dc * 128: ip * 1024 + (dc + 1) * 128],
                                       idn[:, :])
                ins.then_inc(s_pe, 1)
                if dbg != "p1":
                    if i == 15:
                        q_proj(0, 0, 4)
                    elif 16 <= i < 28:
                        q_proj(0, 4 + (i - 16), 5 + (i - 16))
            if dbg == "p1":
                return
            n_pq = 0
            n_ev = 0
            n_gq = 0
            for hh in ((0,) if dbg == "p2p" else (0, 1)):
                if hh == 1:
                    q_proj(1, 0, 16)
                n_pp = n_pp_box[0]
                pe.wait_ge(s_wc, hh * 12 + 12)
                for b in range(4):
                    pe.wait_ge(s_xf[b % 2], 16 * (hh * 2 + b // 2 + 1))
                    for dc in range(4):
                        for j2 in range(2):
                            slot = pSC[n_pp % 4]
                            if n_pp >= 4:
                                pe.wait_ge(s_pk, n_pp - 3)
                            for dinc in range(8):
                                ins = pe.matmul(slot,
                                                wkb[:, dinc * 512 + dc * 128: dinc * 512 + (dc + 1) * 128],
                                                xnfb[:, b % 2, dinc, j2 * 512:(j2 + 1) * 512],
                                                start=(dinc == 0), stop=(dinc == 7))
                            ins.then_inc(s_pp, 1)
                            n_pp += 1
                    if hh == 0 and b == 0:
                        pe.wait_ge(s_v2, 5 + NT1)
                    for tt in range(8):
                        vslot = vslot_of(n_pq)
                        if n_pq >= 4:
                            pe.wait_ge(s_pv, n_pq - 3)
                        for dinc in range(8):
                            ins = pe.matmul(vslot,
                                            xnfb[:, b % 2, dinc, tt * 128:(tt + 1) * 128],
                                            wvb[:, dinc * 512:(dinc + 1) * 512],
                                            start=(dinc == 0), stop=(dinc == 7))
                        ins.then_inc(s_pq, 1)
                        n_pq += 1
                n_pp_box[0] = n_pp
                if dbg == "p2p":
                    continue
                pe.wait_ge(s_pk, (hh + 1) * NQK)
                pe.wait_ge(s_pv, (hh + 1) * NVG)

                def scoreA(e):
                    return PSA[:, (e % 2) * 1024:(e % 2) * 1024 + 512]
                def scoreB(e):
                    return PSA[:, (e % 2) * 1024 + 512:(e % 2 + 1) * 1024]
                def ctxA(J):
                    return PSB[:, (J % 2) * 1024:(J % 2) * 1024 + 512]
                def ctxB(J):
                    return PSB[:, (J % 2) * 1024 + 512:(J % 2 + 1) * 1024]

                pend = None
                for gi in range(hh * 16, (hh + 1) * 16):
                    g = GQ[gi]
                    pi, qc, P = g["pi"], g["qc"], g["P"]
                    nk = len(g["kts"])
                    q0 = pi * 2048 + qc * 512
                    for ki, kt in enumerate(g["kts"]):
                        e = n_ev
                        first, last = ki == 0, ki == nk - 1
                        if e >= 2:
                            pe.wait_ge(s_exa, e - 1)
                            pe.wait_ge(s_exb, e - 1)
                        if ki == 0 and gi >= 1:
                            pe.wait_ge(s_cp, 2 * (gi - 1))
                        elif ki == 1:
                            pe.wait_ge(s_cp, 2 * gi)
                        kA = kf[0:64, pi * 4096 + kt * 128: pi * 4096 + (kt + 1) * 128]
                        kB = kf[64:128, pi * 4096 + kt * 128: pi * 4096 + (kt + 1) * 128]
                        pe.matmul(scoreA(e), kA, qf[0:64, q0:q0 + 512],
                                  start=True, stop=True, tile_position=(0, 0))
                        pe.matmul(scoreB(e), kB, qf[64:128, q0:q0 + 512],
                                  start=True, stop=True,
                                  tile_position=(64, 0)).then_inc(s_qk, 1)
                        n_ev += 1
                        if pend is not None:
                            pe2, gi2, f2, l2, kt2 = pend
                            g2 = GQ[gi2]
                            pe.wait_ge(s_exa, pe2 + 1)
                            pe.wait_ge(s_exb, pe2 + 1)
                            if f2 and gi2 >= 2:
                                pe.wait_ge(s_ml, 2 * (gi2 - 1))
                            pe.matmul(ctxA(gi2)[0:65, :], vt[:, kt2, 2 * g2["pi"], :],
                                      eT[:, (pe2 % 2) * 512:(pe2 % 2) * 512 + 512],
                                      start=f2, stop=l2)
                            ins = pe.matmul(ctxB(gi2)[0:65, :], vt[:, kt2, 2 * g2["pi"] + 1, :],
                                            eT[:, 1024 + (pe2 % 2) * 512: 1024 + (pe2 % 2) * 512 + 512],
                                            start=f2, stop=l2)
                            ins.then_inc(s_av, 1)
                            if l2:
                                for side in range(2):
                                    pe.wait_ge(s_rv, 2 * gi2 + side + 1)
                                    tgt = scoreA(pe2) if side == 0 else scoreB(pe2)
                                    pe.matmul(tgt, ones1[0:1, :], rrow[0:1, 0:512],
                                              start=True, stop=True).then_inc(s_bc, 1)
                        pend = (e, gi, first, last, kt)
                if pend is not None:
                    pe2, gi2, f2, l2, kt2 = pend
                    g2 = GQ[gi2]
                    pe.wait_ge(s_exa, pe2 + 1)
                    pe.wait_ge(s_exb, pe2 + 1)
                    if f2 and gi2 >= 2:
                        pe.wait_ge(s_ml, 2 * (gi2 - 1))
                    pe.matmul(ctxA(gi2)[0:65, :], vt[:, kt2, 2 * g2["pi"], :],
                              eT[:, (pe2 % 2) * 512:(pe2 % 2) * 512 + 512],
                              start=f2, stop=l2)
                    pe.matmul(ctxB(gi2)[0:65, :], vt[:, kt2, 2 * g2["pi"] + 1, :],
                              eT[:, 1024 + (pe2 % 2) * 512: 1024 + (pe2 % 2) * 512 + 512],
                              start=f2, stop=l2).then_inc(s_av, 1)
                    for side in range(2):
                        pe.wait_ge(s_rv, 2 * gi2 + side + 1)
                        tgt = scoreA(pe2) if side == 0 else scoreB(pe2)
                        pe.matmul(tgt, ones1[0:1, :], rrow[0:1, 0:512],
                                  start=True, stop=True).then_inc(s_bc, 1)
                    pend = None
            if dbg in ("p2", "p2c", "p2p"):
                return
            pe.wait_ge(s_wc, 32)
            pe.wait_ge(s_ml, 64)
            pe.wait_ge(s_cp, 64)
            pe.wait_ge(s_exa, NEV)
            pe.wait_ge(s_exb, NEV)
            for qt in range(16):
                slot = [PSA, PSB][qt % 2][:, (qt // 2 % 2) * 1024:((qt // 2 % 2) + 1) * 1024]
                if qt >= 4:
                    pe.wait_ge(s_3v, qt - 3)
                for gp_ in range(8):
                    for n in range(2):
                        ins = pe.matmul(slot[:, n * 512:(n + 1) * 512],
                                        ctxf[:, gp_ * 2048 + qt * 128: gp_ * 2048 + (qt + 1) * 128],
                                        wob[:, gp_ * 1024 + n * 512: gp_ * 1024 + (n + 1) * 512],
                                        start=(gp_ == 0), stop=(gp_ == 7))
                ins.then_inc(s_3m, 1)
            if dbg == "p3":
                return
            n4m = 0
            for T in ((0,) if dbg in ("p4a", "p4b", "p4c", "p4f") else (0, 1)):
                pe.wait_ge(s_3v, 16)
                pe.wait_ge(s_4v, T * 40)
                pe.wait_ge(s_xT, (T + 1) * 16)
                for hb in range(8):
                    pe.wait_ge(s_wa[hb % 2], 32 * (T * 4 + hb // 2 + 1))
                    for hc in range(4):
                        hidc = hb * 4 + hc
                        rot = hidc % 2
                        pw1 = [PSA, PSB][rot][:, 0:1024]
                        pw2 = [PSA, PSB][rot][:, 1024:2048]
                        if hidc >= 2:
                            pe.wait_ge(s_4s, T * 96 + hidc - 1)
                            pe.wait_ge(s_4v, T * 40 + hidc - 1)
                        for w, pw in ((0, pw1), (1, pw2)):
                            for n in range(2):
                                for dinc in range(8):
                                    ins = pe.matmul(
                                        pw[:, n * 512:(n + 1) * 512],
                                        wfA[:, w, hb % 2, dinc, hc * 128:(hc + 1) * 128],
                                        xnT[:, dinc, n * 512:(n + 1) * 512],
                                        start=(dinc == 0), stop=(dinc == 7))
                            ins.then_inc(s_4m, 1)
                            n4m += 1
                if dbg == "p4a":
                    return
                pe.wait_ge(s_4s, T * 96 + 32)
                pe.wait_ge(s_4v, T * 40 + 32)
                for ho in range(32):
                    pe.wait_ge(s_wb[ho % 2], 16 * (T * 32 + ho // 2 + 1))
                    slot = [PSA, PSB][ho % 2][:, (ho // 2 % 2) * 1024:((ho // 2 % 2) + 1) * 1024]
                    if ho >= 4:
                        pe.wait_ge(s_4s, T * 96 + 32 + ho - 3)
                    for n in range(2):
                        for hinc in range(32):
                            ins = pe.matmul(slot[:, n * 512:(n + 1) * 512],
                                            wfB[:, ho % 2, hinc, :],
                                            hbuf[:, hinc, n * 512:(n + 1) * 512],
                                            start=(hinc == 0), stop=(hinc == 31))
                        ins.then_inc(s_4m, 1)
                        n4m += 1
                if dbg == "p4b":
                    return
                pe.wait_ge(s_4s, T * 96 + 64)
                for ho in range(32):
                    pe.wait_ge(s_wb[ho % 2], 16 * (T * 32 + 16 + ho // 2 + 1))
                    slot = [PSA, PSB][ho % 2][:, (ho // 2 % 2) * 1024:((ho // 2 % 2) + 1) * 1024]
                    if ho >= 4:
                        pe.wait_ge(s_4s, T * 96 + 64 + ho - 3)
                    for n in range(2):
                        for hinc in range(32):
                            ins = pe.matmul(slot[:, n * 512:(n + 1) * 512],
                                            wfB[:, ho % 2, hinc, :],
                                            h2b[:, hinc, n * 512:(n + 1) * 512],
                                            start=(hinc == 0), stop=(hinc == 31))
                        ins.then_inc(s_4m, 1)
                        n4m += 1
                if dbg == "p4c":
                    return
                pe.wait_ge(s_4s, T * 96 + 96)
                pe.wait_ge(s_w3, 64 * (T + 1))
                for qt in range(8):
                    gq = T * 8 + qt
                    slot = [PSA, PSB][qt % 2][:, (qt // 2 % 2) * 1024:((qt // 2 % 2) + 1) * 1024]
                    if qt >= 4:
                        pe.wait_ge(s_4v, d_done(T * 8 + qt - 4))
                    for n in range(2):
                        for hinc in range(32):
                            ins = pe.matmul(slot[:, n * 512:(n + 1) * 512],
                                            hbuf[:, hinc, qt * 128:(qt + 1) * 128],
                                            wf3[:, hinc, n * 512:(n + 1) * 512],
                                            start=(hinc == 0), stop=(hinc == 31))
                        ins.then_inc(s_4m, 1)
                        n4m += 1

        @blk.scalar
        def _(sc):
            for i in range(NT1):
                ip = i % 2
                base = ip * 8
                sc.wait_ge(s_x2[ip], 16 * (i // 2 + 1))
                sc.activation(sqs[:, :], xin[:, ip * 1024:(ip + 1) * 1024],
                              AF.Square, bias=0.0, scale=1.0,
                              accum_out=stt[:, base + 1:base + 2]).then_inc(s_se, 1)
                sc.wait_ge(s_vv, i + 1)
                sc.activation(stt[:, base + 3:base + 4], stt[:, base + 2:base + 3],
                              AF.Sqrt, bias=kc[:, 0:1], scale=1.0).then_inc(s_s2, 1)
            if dbg == "p1":
                return
            if dbg == "p2p":
                return
            n_ev = 0
            for hh in (0, 1):
                for gi in range(hh * 16, (hh + 1) * 16):
                    g = GQ[gi]
                    P = g["P"]
                    for ki, kt in enumerate(g["kts"]):
                        e = n_ev
                        sc.wait_ge(s_qk, e + 1)
                        sc.wait_ge(s_vst, e + 1)
                        if e >= 2:
                            sc.wait_ge(s_av, e - 1)
                        sB = PSA[:, (e % 2) * 1024 + 512:(e % 2 + 1) * 1024]
                        sc.activation(eT[:, (e % 2) * 512:(e % 2) * 512 + 512],
                                      tsc[:, e % 2, :], AF.Exp,
                                      bias=p2a[:, P * 32 + kt: P * 32 + kt + 1],
                                      scale=1.0).then_inc(s_exa, 1)
                        sc.activation(eT[:, 1024 + (e % 2) * 512: 1024 + (e % 2) * 512 + 512],
                                      sB, AF.Exp,
                                      bias=p2b[:, P * 32 + kt: P * 32 + kt + 1],
                                      scale=0.125).then_inc(s_exb, 1)
                        n_ev += 1
            if dbg in ("p2", "p2c", "p3"):
                return
            for T in ((0,) if dbg in ("p4a", "p4b", "p4c", "p4f") else (0, 1)):
                for hidc in range(32):
                    hb, hc = hidc // 4, hidc % 4
                    sc.wait_ge(s_4m, T * 208 + hb * 8 + hc * 2 + 1)
                    if hidc >= 2:
                        sc.wait_ge(s_4v, T * 40 + hidc - 1)
                    rot = hidc % 2
                    pw1 = [PSA, PSB][rot][:, 0:1024]
                    sc.activation(silu[:, rot, :], pw1, AF.Silu,
                                  bias=0.0, scale=cv[:, 16 + hidc:17 + hidc]).then_inc(s_4s, 1)
                if dbg == "p4a":
                    return
                for ho in range(32):
                    sc.wait_ge(s_4m, T * 208 + 64 + 2 * ho + 2)
                    slot = [PSA, PSB][ho % 2][:, (ho // 2 % 2) * 1024:((ho // 2 % 2) + 1) * 1024]
                    sc.activation(h2b[:, ho, :], slot, AF.Gelu,
                                  bias=0.0, scale=cv[:, 80 + ho:81 + ho]).then_inc(s_4s, 1)
                if dbg == "p4b":
                    return
                for ho in range(32):
                    sc.wait_ge(s_4m, T * 208 + 128 + 2 * ho + 2)
                    slot = [PSA, PSB][ho % 2][:, (ho // 2 % 2) * 1024:((ho // 2 % 2) + 1) * 1024]
                    sc.activation(hbuf[:, ho, :], slot, AF.Gelu,
                                  bias=0.0, scale=cv[:, 112 + ho:113 + ho]).then_inc(s_4s, 1)

        @blk.vector
        def _(ve):
            ve.memset(kc[:, 0:1], EPS)
            ve.memset(idn[:, :], 1.0).then_inc(s_v2, 1)
            ve.memset(ones1[0:1, :], 1.0).then_inc(s_v2, 1)
            ve.wait_ge(s_ge, 1)
            for P in range(8):
                hB = PAIRS[P][1]
                ve.tensor_scalar(p2b[:, P * 32:(P + 1) * 32], io32[:, :],
                                 SLOPES32[hB], -M_H[hB], ALU.mult, ALU.add)
            for P in (1, 2, 3, 4, 5, 6, 7, 0):
                hA = PAIRS[P][0]
                if P == 0:
                    ve.drain()
                ve.tensor_scalar(p2a[:, P * 32:(P + 1) * 32], io32[:, :],
                                 SLOPES32[hA], -M_H[hA], ALU.mult, ALU.add)
            for r in range(3):
                ve.wait_ge(s_pe, r + 1)
                tgt = [bvb, bob, s3b][r]
                ve.tensor_copy(tgt[:, 0:512], pB[:, :])
                ve.tensor_copy(tgt[:, 512:1024], pB2[:, :]).then_inc(s_v2, 1)
            for it in range(NT1 + 2):
                if it < NT1:
                    i = it
                    ip = i % 2
                    base = ip * 8
                    ve.wait_ge(s_x2[ip], 16 * (i // 2 + 1))
                    xi = xin[:, ip * 1024:(ip + 1) * 1024]
                    ve.reduce_sum(stt[:, base + 0:base + 1], xi, axis=AX.X)
                    ve.drain()
                    ve.tensor_scalar(stt[:, base + 5:base + 6], stt[:, base + 0:base + 1],
                                     1.0 / 1024, None, ALU.mult)
                    ve.drain()
                    ve.tensor_mul(stt[:, base + 6:base + 7], stt[:, base + 5:base + 6],
                                  stt[:, base + 5:base + 6])
                    ve.drain()
                    ve.wait_ge(s_se, i + 1)
                    ve.tensor_scalar(stt[:, base + 2:base + 3], stt[:, base + 1:base + 2],
                                     1.0 / 1024, stt[:, base + 6:base + 7],
                                     ALU.mult, ALU.subtract).then_inc(s_vv, 1)
                if 1 <= it <= NT1:
                    i = it - 1
                    ip = i % 2
                    base = ip * 8
                    xi = xin[:, ip * 1024:(ip + 1) * 1024]
                    ve.wait_ge(s_s2, i + 1)
                    ve.reciprocal(stt[:, base + 4:base + 5], stt[:, base + 3:base + 4])
                    ve.drain()
                    if i >= 2:
                        ve.wait_ge(s_pe, 3 + i - 1)
                    ve.tensor_scalar(xnt[:, ip * 1024:(ip + 1) * 1024], xi,
                                     stt[:, base + 5:base + 6], stt[:, base + 4:base + 5],
                                     ALU.subtract, ALU.mult).then_inc(s_ve, 1)
                if 2 <= it:
                    i = it - 2
                    ip = i % 2
                    ve.wait_ge(s_pe, 3 + i + 1)
                    g, j0 = i // 4, (i % 4) * 128
                    if j0 == 0 and g >= 2:
                        ve.wait_ge(s_sb[g % 2], 16 * ((g - 2) // 2 + 1))
                    ve.tensor_copy(xs[:, g % 2, :, j0:j0 + 128],
                                   pT[ip][:, :].rearrange("p (d j) -> p d j", d=8)).then_inc(s_v2, 1)
                    if dbg != "p1" and 16 <= i < 32:
                        j = i - 16
                        dc = j // 4
                        if j == 0:
                            ve.wait_ge(s_cv, 16)
                        ve.wait_ge(s_pp, j + 1)
                        ve.tensor_scalar(qf[:, dc * 2048 + (j % 4) * 512: dc * 2048 + (j % 4 + 1) * 512],
                                         pSC[j % 4], cv[:, 8 + dc: 9 + dc],
                                         None, ALU.add).then_inc(s_pk, 1)
            if dbg == "p1":
                return
            n_pp = 16
            n_pq = 0
            n_ev = 0
            n_gq = 0
            for hh in ((0,) if dbg == "p2p" else (0, 1)):
                if hh == 1:
                    ve.wait_ge(s_av, EV_HALF0_END)
                else:
                    ve.wait_ge(s_s2, NT1)
                    ve.wait_ge(s_sb[0], 16 * 6)
                    ve.wait_ge(s_sb[1], 16 * 6)
                ve.memset(vt[:, :, :, :].rearrange("p a b c -> p (a b c)"), 1.0)
                if hh == 1:
                    for dc in range(4):
                        for tc in range(4):
                            ve.wait_ge(s_pp, n_pp + 1)
                            ve.tensor_scalar(qf[:, dc * 2048 + tc * 512: dc * 2048 + (tc + 1) * 512],
                                             pSC[n_pp % 4], cv[:, 12 + dc: 13 + dc],
                                             None, ALU.add).then_inc(s_pk, 1)
                            n_pp += 1
                for b in range(4):
                    for dc in range(4):
                        for j2 in range(2):
                            ve.wait_ge(s_pp, n_pp + 1)
                            col = dc * 4096 + b * 1024 + j2 * 512
                            ve.tensor_scalar(kf[:, col:col + 512], pSC[n_pp % 4],
                                             cv[:, hh * 4 + dc: hh * 4 + dc + 1],
                                             None, ALU.add).then_inc(s_pk, 1)
                            n_pp += 1
                    for tt in range(8):
                        ve.wait_ge(s_pq, n_pq + 1)
                        vslot = vslot_of(n_pq)
                        tg = b * 8 + tt
                        ve.tensor_tensor(vt[:, tg, :, 0:64],
                                         vslot.rearrange("p (a c) -> p a c", a=8),
                                         bvb[:, hh * 512:(hh + 1) * 512]
                                         .rearrange("p (a c) -> p a c", a=8),
                                         ALU.add).then_inc(s_pv, 1)
                        n_pq += 1
                if dbg == "p2p":
                    continue
                for gi in range(hh * 16, (hh + 1) * 16):
                    g = GQ[gi]
                    P, pi, qc = g["P"], g["pi"], g["qc"]
                    for ki, kt in enumerate(g["kts"]):
                        e = n_ev
                        ve.wait_ge(s_qk, e + 1)
                        if ki == 0:
                            ve.wait_ge(s_ub, 16 * (gi + 1))
                        if e >= 2:
                            ve.wait_ge(s_exa, e - 1)
                        off = 3968 - kt * 128
                        ve.scalar_tensor_tensor(tsc[:, e % 2, :],
                                                PSA[:, (e % 2) * 1024:(e % 2) * 1024 + 512],
                                                0.125, ub[:, off:off + 512],
                                                ALU.mult, ALU.add).then_inc(s_vst, 1)
                        n_ev += 1
                    eL = n_ev - 1
                    cA = PSB[:, (gi % 2) * 1024:(gi % 2) * 1024 + 512]
                    cB = PSB[:, (gi % 2) * 1024 + 512:(gi % 2 + 1) * 1024]
                    bA = PSA[:, (eL % 2) * 1024:(eL % 2) * 1024 + 512]
                    bB = PSA[:, (eL % 2) * 1024 + 512:(eL % 2 + 1) * 1024]
                    for side, pC, pBc in ((0, cA, bA), (1, cB, bB)):
                        ve.wait_ge(s_av, n_ev)
                        ve.wait_ge(s_bc, 2 * gi + side)
                        ve.reciprocal(rrow[0:1, 0:512], pC[64:65, :]).then_inc(s_rv, 1)
                        ve.wait_ge(s_bc, 2 * gi + side + 1)
                        ve.tensor_copy(rbc[0:64, 0:512], pBc[0:64, :]).then_inc(s_cp, 1)
                        ve.drain()
                        gp_ = 4 * hh + pi
                        ve.tensor_tensor(
                            ctxf[side * 64:(side + 1) * 64,
                                 gp_ * 2048 + qc * 512: gp_ * 2048 + (qc + 1) * 512],
                            pC[0:64, :], rbc[0:64, 0:512], ALU.mult).then_inc(s_ml, 1)
                    n_gq += 1
            if dbg in ("p2", "p2c", "p2p"):
                return
            for qt in range(16):
                slot = [PSA, PSB][qt % 2][:, (qt // 2 % 2) * 1024:((qt // 2 % 2) + 1) * 1024]
                ve.wait_ge(s_3m, qt + 1)
                ve.wait_ge(s_3x2[qt % 2], 16 * (qt // 2 + 1))
                if qt >= 2:
                    ve.wait_ge(s_3a2[qt % 2], 16 * ((qt - 2) // 2 + 1))
                ao = aout[:, (qt % 2) * 1024:(qt % 2 + 1) * 1024]
                ve.tensor_tensor(ao, slot, bob[:, :], ALU.add)
                ve.drain()
                ve.tensor_tensor(ao, ao, xacc[:, (qt % 2) * 1024:(qt % 2 + 1) * 1024],
                                 ALU.add).then_inc(s_3v, 1)
            if dbg == "p3":
                return
            for T in ((0,) if dbg in ("p4a", "p4b", "p4c", "p4f") else (0, 1)):
                for hidc in range(32):
                    hb, hc = hidc // 4, hidc % 4
                    if T == 0 and hidc == 0:
                        ve.wait_ge(s_3a2[0], 16 * 8)
                        ve.wait_ge(s_3a2[1], 16 * 8)
                    ve.wait_ge(s_4m, T * 208 + hb * 8 + hc * 2 + 2)
                    ve.wait_ge(s_4s, T * 96 + hidc + 1)
                    rot = hidc % 2
                    pw2 = [PSA, PSB][rot][:, 1024:2048]
                    ve.scalar_tensor_tensor(hbuf[:, hidc, :], pw2,
                                            cv[:, 48 + hidc:49 + hidc],
                                            silu[:, rot, :],
                                            ALU.mult, ALU.mult).then_inc(s_4v, 1)
                if dbg in ("p4a", "p4b", "p4c"):
                    return
                for qt in range(8):
                    gq = T * 8 + qt
                    slot = [PSA, PSB][qt % 2][:, (qt // 2 % 2) * 1024:((qt // 2 % 2) + 1) * 1024]
                    ve.wait_ge(s_4m, T * 208 + 192 + 2 * qt + 2)
                    ve.wait_ge(s_ac[gq % 2], 16 * (gq // 2 + 1))
                    if gq >= 2:
                        ve.wait_ge(s_4o2[gq % 2], 16 * ((gq - 2) // 2 + 1))
                    uo = uout[:, gq % 2, :]
                    ve.tensor_tensor(uo, slot, s3b[:, :], ALU.mult)
                    ve.drain()
                    ve.tensor_tensor(uo, uo, acci[:, gq % 2, :], ALU.add).then_inc(s_4v, 1)


    es.close()
    return nc


def make_inputs(ins):
    x = np.asarray(ins["L0_attn_inputs"], dtype=np.float32)
    f32 = np.float32
    cvec = np.zeros((128, 144), dtype=f32)
    cvec[:, 0:8] = np.asarray(ins["L0_attn_bk"], f32).reshape(8, 128).T
    cvec[:, 8:16] = np.asarray(ins["L0_attn_bq"], f32).reshape(8, 128).T
    cvec[:, 16:48] = np.asarray(ins["L0_ffn_w1_s"], f32).reshape(32, 128).T
    cvec[:, 48:80] = np.asarray(ins["L0_ffn_w2_s"], f32).reshape(32, 128).T
    cvec[:, 80:112] = np.asarray(ins["L0_ffn_h1_s"], f32).reshape(32, 128).T
    cvec[:, 112:144] = np.asarray(ins["L0_ffn_h2_s"], f32).reshape(32, 128).T
    cvec2 = np.concatenate([np.asarray(ins["L0_attn_bv"], f32),
                            np.asarray(ins["L0_attn_bo"], f32),
                            np.asarray(ins["L0_ffn_w3_s"], f32)]).reshape(1, 3072)

    def slab_a(w):
        wT = np.ascontiguousarray(np.asarray(w, dtype=f32).T)
        return np.ascontiguousarray(wT.reshape(8, 128, 8, 512).transpose(2, 0, 1, 3))
    def slab_h(w):
        ww = np.asarray(w, dtype=f32)
        return np.ascontiguousarray(ww.reshape(32, 128, 32, 128).transpose(0, 3, 2, 1))
    def slab_3(w):
        wT = np.ascontiguousarray(np.asarray(w, dtype=f32).T)
        return np.ascontiguousarray(wT.reshape(32, 128, 1024))

    shared = {
        "Wq": np.ascontiguousarray(np.asarray(ins["L0_attn_Wq"], f32)),
        "Wk": np.ascontiguousarray(np.asarray(ins["L0_attn_Wk"], f32)),
        "Wv": np.ascontiguousarray(np.asarray(ins["L0_attn_Wv"], f32)),
        "Wo": np.ascontiguousarray(np.asarray(ins["L0_attn_Wo"], f32)),
        "cvec": cvec, "cvec2": cvec2,
        "w1sl": slab_a(ins["L0_ffn_w1_w"]), "w2sl": slab_a(ins["L0_ffn_w2_w"]),
        "h1sl": slab_h(ins["L0_ffn_h1_w"]), "h2sl": slab_h(ins["L0_ffn_h2_w"]),
        "w3sl": slab_3(ins["L0_ffn_w3_w"]),
    }
    in_maps = []
    p = np.arange(128)[:, None]
    for c in range(8):
        b, qbase = c // 2, (c % 2) * QC
        m = dict(shared)
        m["x_seq"] = np.ascontiguousarray(x[b])
        m["x_own"] = np.ascontiguousarray(x[b, qbase:qbase + QC])
        for h in range(0, H, 2):
            j = np.arange(UW[h])[None, :]
            dgrid = p + (3968 - qbase) - j
            T = (np.float32(SLOPES32[h]) * dgrid.astype(f32)).astype(f32)
            eps = T.astype(np.float64) - np.float64(SLOPES32[h]) * dgrid
            m[f"U{h}"] = eps.astype(f32)
        in_maps.append(m)
    return in_maps


def kernel(**inputs):
    in_maps = make_inputs(inputs)
    nc = build_nc()
    res = run_bass_kernel_spmd(nc, in_maps, core_ids=list(range(8)))
    o = np.empty((B, S, E), dtype=np.float32)
    for c in range(8):
        b, qbase = c // 2, (c % 2) * QC
        o[b, qbase:qbase + QC] = res.results[c]["out"]
    return o
